# version 16
# baseline (speedup 1.0000x reference)
import math
from contextlib import ExitStack

import numpy as np

import concourse.bass as bass
import concourse.mybir as mybir
import concourse.bass_utils as _bass_utils
from concourse.bass_utils import run_bass_kernel_spmd

if not getattr(_bass_utils, "_ldw_opt_patched", False):
    _orig_run_command = _bass_utils.run_command

    def _run_command_ldw(argv, **kw):
        argv = ["--enable-ldw-opt=true" if a == "--enable-ldw-opt=false" else a
                for a in argv]
        return _orig_run_command(argv, **kw)

    _bass_utils.run_command = _run_command_ldw
    _bass_utils._ldw_opt_patched = True

F32 = mybir.dt.float32
BF16 = mybir.dt.bfloat16
AF = mybir.ActivationFunctionType
ALU = mybir.AluOpType
AX = mybir.AxisListType

NCORES = 8


def default_cfg():
    return dict(D=4096, SB=4096, TOWN=2048, H=32, HG=2, QC=512, KTG=2,
                TP=512, TCH=512, TC=512, ER=256, EPS=1e-5)


class Sem:
    def __init__(self, es, nc, name):
        self.h = es.enter_context(nc.semaphore(name))
        self.n = 0


class Prog:

    def __init__(self):
        self.streams = {k: [] for k in ("pool", "sp", "pe", "act", "dve")}

    def op(self, eng, fn, waits=(), incs=()):
        for s, a in incs:
            s.n += a
        waits = [(s, v) for (s, v) in waits if v > 0]
        self.streams[eng].append((tuple(waits), fn, tuple(incs)))

    def emit(self, blk):
        def body(name):
            def run(e):
                for waits, fn, incs in self.streams[name]:
                    for s, v in waits:
                        e.wait_ge(s.h, v)
                    inst = fn(e)
                    for s, a in incs:
                        inst = inst.then_inc(s.h, a)
            return run
        if self.streams["pool"]:
            blk.gpsimd(body("pool"))
        if self.streams["sp"]:
            blk.sync(body("sp"))
        if self.streams["pe"]:
            blk.tensor(body("pe"))
        if self.streams["act"]:
            blk.scalar(body("act"))
        if self.streams["dve"]:
            blk.vector(body("dve"))


def _finish_debug(nc, out_h, D, TOWN):
    with ExitStack() as esz:
        z = esz.enter_context(nc.sbuf_tensor("zfill", [128, TOWN], mybir.dt.float32))
        sz = esz.enter_context(nc.semaphore("szf"))
        so = esz.enter_context(nc.semaphore("sof"))
        with nc.Block() as blkz:
            pz = Prog()
            szs = Sem.__new__(Sem); szs.h = sz; szs.n = 0
            sos = Sem.__new__(Sem); sos.h = so; sos.n = 0
            pz.op("pool", lambda e: e.memset(z[:, :], 0.0), incs=[(szs, 1)])
            for dc in range(D // 128):
                pz.op("sp", (lambda dc: lambda e: e.dma_start(
                    out_h.ap()[dc * 128:(dc + 1) * 128, :], z[:, :]))(dc),
                    waits=[(szs, 1)] if dc == 0 else [], incs=[(sos, 16)])
            pz.op("sp", lambda e: e.nop(), waits=[(sos, sos.n)])
            pz.emit(blkz)


def _pad_psum(shape, dt):
    per = 2048 // mybir.dt.size(dt)
    cols = ((shape[1] + per - 1) // per) * per
    return [shape[0], cols]


def build_program(cfg):
    D = cfg["D"]; SB = cfg["SB"]; TOWN = cfg["TOWN"]; H = cfg["H"]
    HG = cfg["HG"]; QC = cfg["QC"]; KTG = cfg["KTG"]; TP = cfg["TP"]
    TCH = cfg["TCH"]; TC = cfg["TC"]; ER = cfg["ER"]; EPS = cfg["EPS"]
    TC_DBG = cfg["TC"]
    DC = D // 128
    NG = H // HG
    NTP = SB // TP
    NOWNP = TOWN // TP
    NQC = TOWN // QC
    NKT = SB // 128
    NKTG = NKT // KTG
    NTCH = SB // TCH
    NTC = TOWN // TC
    QSUB = QC // 128
    ERC = ER // 128
    VPT = TP // 128
    SCALE = 1.0 / math.sqrt(128.0)
    PHASES = cfg.get("PHASES", "ABD")
    ZB = cfg.get("ZB", False)
    NOATTN = cfg.get("NOATTN", False)
    NGL = cfg.get("NGL", 0)
    QP = cfg.get("QP", "kvq")
    NUNITS = NG * HG * NQC
    assert ERC == 2 and KTG == 2 and QSUB <= 4

    nc = bass.Bass("TRN2", target_bir_lowering=False, dynamic_dma_scratch_size=16384)

    xT = nc.dram_tensor("xT", [D, SB], F32, kind="ExternalInput")
    WqT = nc.dram_tensor("WqT", [D, D], F32, kind="ExternalInput")
    WkT = nc.dram_tensor("WkT", [D, D], F32, kind="ExternalInput")
    WvT = nc.dram_tensor("WvT", [D, D], F32, kind="ExternalInput")
    WoT = nc.dram_tensor("WoT", [D, D], F32, kind="ExternalInput")
    WfT = nc.dram_tensor("WfT", [D, D], F32, kind="ExternalInput")
    AsT = nc.dram_tensor("AsT", [D, ER], F32, kind="ExternalInput")
    BfL = nc.dram_tensor("BfL", [ER, D], F32, kind="ExternalInput")
    bias_h = nc.dram_tensor("bias", [128, D // 128], F32, kind="ExternalInput")
    cs_h = nc.dram_tensor("coefsc", [32, 2], F32, kind="ExternalInput")

    dbg = "ExternalOutput" if cfg.get("DBG") else "Internal"
    DBG2 = cfg.get("DBG2", False)
    if DBG2:
        ps_dbg = nc.dram_tensor("ps_dbg", [128, TC_DBG], F32, kind="ExternalOutput")
        xs_dbg = nc.dram_tensor("xs_dbg", [128, TC_DBG], F32, kind="ExternalOutput")
        ot_dbg = nc.dram_tensor("ot_dbg", [D, TC_DBG], BF16, kind="ExternalOutput")
        x1_dbg = nc.dram_tensor("x1_dbg", [D, TOWN], F32, kind="ExternalOutput")
        h2_dbg = nc.dram_tensor("h2_dbg", [D, TOWN], BF16, kind="ExternalOutput")
        t1_dbg = nc.dram_tensor("t1_dbg", [ER, TOWN], BF16, kind="ExternalOutput")
    hT_scr = nc.dram_tensor("hT_scr", [D, SB], BF16, kind=dbg)
    O_scr = nc.dram_tensor("O_scr", [D, TOWN], BF16, kind=dbg)
    VT_scr = nc.dram_tensor("VT_scr", [D, SB], BF16)
    Wo_scr = nc.dram_tensor("Wo_scr", [D, D], BF16, kind=("ExternalOutput" if cfg.get("DBG3") else "Internal"))
    Wf_scr = nc.dram_tensor("Wf_scr", [D, D], BF16, kind=("ExternalOutput" if cfg.get("DBG3") else "Internal"))
    bias_scr = nc.dram_tensor("bias_scr", [NUNITS + 8, 128], F32)
    out_h = nc.dram_tensor("out", [D, TOWN], F32, kind="ExternalOutput")

    es = ExitStack()
    sem = lambda name: Sem(es, nc, name)

    ones_col_bf = es.enter_context(nc.sbuf_tensor("ones_col_bf", [128, 1], BF16))
    ones_row_f = es.enter_context(nc.sbuf_tensor("ones_row_f", [1, 128], F32))
    ones_col_f = es.enter_context(nc.sbuf_tensor("ones_col_f", [128, 1], F32))
    zeros_col = es.enter_context(nc.sbuf_tensor("zeros_col", [128, 1], F32))
    eps_t = es.enter_context(nc.sbuf_tensor("eps_t", [1, 1], F32))

    with nc.Block() as blk0:
        p0 = Prog()
        s0 = sem("s0")
        p0.op("pool", lambda e: e.memset(ones_col_bf[:, :], 1.0), incs=[(s0, 1)])
        p0.op("pool", lambda e: e.memset(ones_row_f[:, :], 1.0), incs=[(s0, 1)])
        p0.op("pool", lambda e: e.memset(ones_col_f[:, :], 1.0), incs=[(s0, 1)])
        p0.op("pool", lambda e: e.memset(zeros_col[:, :], 0.0), incs=[(s0, 1)])
        p0.op("pool", lambda e: e.memset(eps_t[:, :], EPS), incs=[(s0, 1)])
        p0.op("pool", lambda e: e.nop(), waits=[(s0, 4)])
        p0.emit(blk0)

    with ExitStack() as esA:
        sbA = lambda n, s, d: esA.enter_context(nc.sbuf_tensor(n, s, d))
        psA = lambda n, s, d: esA.enter_context(nc.psum_tensor(n, _pad_psum(s, d), d))
        xbf = [sbA(f"xbf{i}", [128, DC * TCH], BF16) for i in range(2)]
        QTR = DC // 4
        xf32 = [sbA(f"xf32{i}", [128, QTR * TCH], F32) for i in range(2)]
        hbf = [sbA(f"hbf{i}", [128, DC * TCH], BF16) for i in range(2)]
        xsq = [sbA(f"xsq{i}", [128, TCH], BF16) for i in range(2)]
        a_bc = [sbA(f"a_bc{i}", [128, TCH], F32) for i in range(2)]
        b_bc = [sbA(f"b_bc{i}", [128, TCH], F32) for i in range(2)]
        rowb = sbA("rowb", [1, 5 * TCH], F32)
        sx_ps = psA("sx_ps", [1, TCH], F32)
        sx2_ps = psA("sx2_ps", [1, TCH], F32)
        bc_ps = psA("bc_ps", [128, TCH], F32)

        s_xf = [sem("s_xf0"), sem("s_xf1")]; s_xcast = sem("s_xcast")
        s_xq = sem("s_xq"); s_sx = sem("s_sx")
        s_row = sem("s_row"); s_bcm = sem("s_bcm"); s_bce = sem("s_bce")
        s_h = sem("s_h"); s_hw = sem("s_hw")

        pA = Prog()
        NSPL = max(1, DC // 8)
        SPW = DC // NSPL

        h_done = []
        sx_free = 0
        for t in range(NTCH):
            par = t % 2
            c0 = t * TCH
            xw_prev = [(s_sx, 2 * DC * (t - 1))] if t >= 2 else []
            hv_prev = [(s_h, h_done[t - 2])] if t >= 2 else []
            for q in range(4):
                Q = t * 4 + q
                pA.op("act" if Q % 2 == 0 else "sp",
                      (lambda c0, q, Q: lambda e: e.dma_start(
                          xf32[Q % 2][:, :].rearrange("p (a c) -> p a c", a=QTR),
                          xT.ap()[q * QTR * 128:(q + 1) * QTR * 128, c0:c0 + TCH]
                          .rearrange("(a p) c -> p a c", p=128)))(c0, q, Q),
                      waits=[(s_xcast, max(0, Q - 1))], incs=[(s_xf[Q % 2], 16)])
                cwaits = [(s_xf[Q % 2], 16 * (Q // 2 + 1))]
                if q == 0:
                    cwaits += xw_prev + hv_prev
                pA.op("act", (lambda par, q, Q: lambda e: e.activation(
                    xbf[par][:, q * QTR * TCH:(q + 1) * QTR * TCH], xf32[Q % 2][:, :],
                    AF.Copy, bias=0.0, scale=1.0))(par, q, Q),
                    waits=cwaits, incs=[(s_xcast, 1)])
            for dc in range(DC):
                gi = t * DC + dc
                sq = dc % 2
                xcw = 4 * t + dc // QTR + 1
                pA.op("dve", (lambda par, dc, sq, xcw: lambda e: e.tensor_mul(
                    xsq[sq][:, :], xbf[par][:, dc * TCH:(dc + 1) * TCH],
                    xbf[par][:, dc * TCH:(dc + 1) * TCH]))(par, dc, sq, xcw),
                    waits=[(s_xcast, xcw)] + ([(s_sx, 2 * (gi - 1))] if gi >= 2 else []),
                    incs=[(s_xq, 1)])
                pA.op("pe", (lambda par, dc, xcw: lambda e: e.matmul(
                    sx_ps[:, 0:TCH], ones_col_bf[:, :], xbf[par][:, dc * TCH:(dc + 1) * TCH],
                    start=(dc == 0), stop=(dc == DC - 1)))(par, dc, xcw),
                    waits=[(s_xcast, xcw)] + ([(s_row, sx_free)] if dc == 0 else []),
                    incs=[(s_sx, 1)])
                pA.op("pe", (lambda dc, sq: lambda e: e.matmul(
                    sx2_ps[:, 0:TCH], ones_col_bf[:, :], xsq[sq][:, :],
                    start=(dc == 0), stop=(dc == DC - 1)))(dc, sq),
                    waits=[(s_xq, gi + 1)],
                    incs=[(s_sx, 1)])
            smm = s_sx.n
            m_r, v_r, r_r, b_r, t_r = [slice(i * TCH, (i + 1) * TCH) for i in range(5)]
            pA.op("dve", lambda e: e.tensor_scalar(rowb[0:1, m_r], sx_ps[:, 0:TCH], 1.0 / D, None, ALU.mult),
                  waits=[(s_sx, smm)], incs=[(s_row, 1)])
            pA.op("dve", lambda e: e.tensor_scalar(rowb[0:1, v_r], sx2_ps[:, 0:TCH], 1.0 / D, None, ALU.mult),
                  incs=[(s_row, 1)])
            sx_free = s_row.n
            pA.op("dve", lambda e: e.tensor_mul(rowb[0:1, t_r], rowb[0:1, m_r], rowb[0:1, m_r]),
                  waits=[(s_row, s_row.n)], incs=[(s_row, 1)])
            pA.op("dve", lambda e: e.tensor_sub(rowb[0:1, v_r], rowb[0:1, v_r], rowb[0:1, t_r]),
                  waits=[(s_row, s_row.n)], incs=[(s_row, 1)])
            rw0 = s_row.n
            pA.op("act", lambda e: e.activation(rowb[0:1, t_r], rowb[0:1, v_r], AF.Ln,
                                                bias=eps_t[0:1, 0:1], scale=1.0),
                  waits=[(s_row, rw0)], incs=[(s_row, 1)])
            pA.op("act", lambda e: e.activation(rowb[0:1, r_r], rowb[0:1, t_r], AF.Exp,
                                                bias=0.0, scale=-0.5),
                  waits=[(s_row, s_row.n)], incs=[(s_row, 1)])
            rw1 = s_row.n
            pA.op("dve", lambda e: e.tensor_mul(rowb[0:1, b_r], rowb[0:1, m_r], rowb[0:1, r_r]),
                  waits=[(s_row, rw1)], incs=[(s_row, 1)])
            pA.op("dve", lambda e: e.tensor_scalar(rowb[0:1, b_r], rowb[0:1, b_r], -1.0, None, ALU.mult),
                  waits=[(s_row, s_row.n)], incs=[(s_row, 1)])
            rw2 = s_row.n
            pA.op("pe", lambda e: e.matmul(bc_ps[:, 0:TCH], ones_row_f[0:1, :], rowb[0:1, r_r],
                                           start=True, stop=True),
                  waits=[(s_row, rw2), (s_bce, 2 * t)], incs=[(s_bcm, 1)])
            pA.op("dve", (lambda par: lambda e: e.tensor_copy(a_bc[par][:, :], bc_ps[:, 0:TCH]))(par),
                  waits=[(s_bcm, 2 * t + 1)], incs=[(s_bce, 1)])
            pA.op("pe", lambda e: e.matmul(bc_ps[:, 0:TCH], ones_row_f[0:1, :], rowb[0:1, b_r],
                                           start=True, stop=True),
                  waits=[(s_bce, 2 * t + 1)], incs=[(s_bcm, 1)])
            pA.op("dve", (lambda par: lambda e: e.tensor_copy(b_bc[par][:, :], bc_ps[:, 0:TCH]))(par),
                  waits=[(s_bcm, 2 * t + 2)], incs=[(s_bce, 1)])
            bce = s_bce.n
            hw_guard = [(s_hw, 16 * NSPL * t)] if t >= 2 else []
            for dc in range(DC):
                pA.op("dve", (lambda par, dc: lambda e: e.tensor_mul(
                    hbf[par][:, dc * TCH:(dc + 1) * TCH], xbf[par][:, dc * TCH:(dc + 1) * TCH],
                    a_bc[par][:, :]))(par, dc),
                    waits=([(s_bce, bce)] + hw_guard) if dc == 0 else [], incs=[(s_h, 1)])
                pA.op("dve", (lambda par, dc: lambda e: e.tensor_add(
                    hbf[par][:, dc * TCH:(dc + 1) * TCH], hbf[par][:, dc * TCH:(dc + 1) * TCH],
                    b_bc[par][:, :]))(par, dc),
                    waits=[(s_h, s_h.n)], incs=[(s_h, 1)])
            pA.op("dve", lambda e: e.drain(), incs=[(s_h, 1)])
            h_done.append(s_h.n)
            for spl in range(NSPL):
                d0, d1 = spl * SPW, (spl + 1) * SPW
                pA.op("sp", (lambda par, c0, d0, d1: lambda e: e.dma_start(
                    hT_scr.ap()[d0 * 128:d1 * 128, c0:c0 + TCH].rearrange("(a p) c -> p a c", p=128),
                    hbf[par][:, d0 * TCH:d1 * TCH].rearrange("p (a c) -> p a c", a=d1 - d0)))(par, c0, d0, d1),
                    waits=[(s_h, h_done[-1])] if spl == 0 else [], incs=[(s_hw, 16)])
        pA.op("sp", lambda e: e.nop(),
              waits=[(s_hw, 16 * NSPL * NTCH), (s_xf[0], s_xf[0].n), (s_xf[1], s_xf[1].n)])

        with nc.Block() as blkA:
            pA.emit(blkA)

    if "B" not in PHASES:
        _finish_debug(nc, out_h, D, TOWN)
        es.close()
        return nc

    UPG = HG * NQC
    with ExitStack() as esB:
        sbB = lambda n, s, d: esB.enter_context(nc.sbuf_tensor(n, s, d))
        psB = lambda n, s, d: esB.enter_context(nc.psum_tensor(n, _pad_psum(s, d), d))
        hp = [sbB(f"hp{i}", [128, DC * TP], BF16) for i in range(2)]
        wq_b = sbB("wq_b", [128, DC * HG * 128], BF16)
        wk_b = sbB("wk_b", [128, DC * HG * 128], BF16)
        wv_b = sbB("wv_b", [128, DC * HG * 128], BF16)
        KT = [[sbB(f"KT{p}{h}", [128, SB], BF16) for h in range(HG)] for p in range(2)]
        QT = [[sbB(f"QT{p}{h}", [128, TOWN], BF16) for h in range(HG)] for p in range(2)]
        VA = [[sbB(f"VA{p}{h}", [128, NKT * 128], BF16) for h in range(HG)] for p in range(2)]
        AT = [sbB(f"AT{i}", [128, KTG * QC], BF16) for i in range(2)]
        acc = sbB("acc", [128, KTG * QC], F32)
        vstg = [sbB(f"vstg{i}", [128, TP], BF16) for i in range(2)]
        ostg = [sbB(f"ostg{i}", [128, QC], BF16) for i in range(2)]
        negm = sbB("negm", [1, 4], F32)
        dmx = sbB("dmx", [1, 1], F32)
        lnm = sbB("lnm", [1, 1], F32)
        biasr = [sbB(f"biasr{i}", [128, 1], F32) for i in range(4)]
        rdn_bc = sbB("rdn_bc", [128, QC], F32)

        st_ps = [psB(f"st_ps{i}", [128, KTG * QC], F32) for i in range(2)]
        qk_ps = [psB(f"qk_ps{i}", [128, TP], F32) for i in range(2)]
        ov_ps = [psB(f"ov_ps{i}", [128, QC], F32) for i in range(2)]

        s_hp = [sem("s_hp0"), sem("s_hp1")]
        s_w = [[sem(f"s_w{p_}{q_}") for q_ in range(2)] for p_ in range(3)]
        s_qm = sem("s_qm"); s_qe = sem("s_qe"); s_stm = sem("s_stm")
        s_ex = sem("s_ex"); s_eu = sem("s_eu"); s_gm = sem("s_gm"); s_bw = sem("s_bw")
        s_br = sem("s_br"); s_av = sem("s_av"); s_avk = sem("s_avk"); s_oe = sem("s_oe")
        s_ob = [sem(f"s_ob{i_}") for i_ in range(2)]
        s_vw = [sem("s_vw0"), sem("s_vw1")]; s_vt = sem("s_vt")
        s_dacc = sem("s_dacc"); s_dn = sem("s_dn"); s_tp = sem("s_tp")
        s_w32 = sem("s_w32")

        pB = Prog()
        NSPL = max(1, DC // 8)
        SPW = DC // NSPL

        st_user = [(0, 0), (0, 0)]
        at_guard = [0, 0]
        qk_user = [0, 0]
        state = {"unit": 0, "bank": 1}
        qe_step_done = {}
        vt_done = {}
        hp_free = {}
        w_free = {}
        vw_cnt = [0, 0]

        def load_w(g, which):
            wt, buf = [(WqT, wq_b), (WkT, wk_b), (WvT, wv_b)][which]
            wfv = w_free.get((which, g - 1), 0)
            for spl in range(NSPL):
                d0, d1 = spl * SPW, (spl + 1) * SPW
                pB.op("pool", (lambda wt, buf, g, d0, d1: lambda e: e.dma_start(
                    buf[:, d0 * HG * 128:d1 * HG * 128].rearrange("p (a c) -> p a c", a=d1 - d0),
                    wt.ap()[d0 * 128:d1 * 128, g * HG * 128:(g + 1) * HG * 128]
                    .rearrange("(a p) c -> p a c", p=128)))(wt, buf, g, d0, d1),
                    waits=[(s_qm, wfv)] if spl == 0 else [],
                    incs=[(s_w[which][g % 2], 16)])

        def load_piece(g, tp):
            idx = g * NTP + tp
            for spl in range(NSPL):
                d0, d1 = spl * SPW, (spl + 1) * SPW
                pB.op("sp", (lambda idx, tp, d0, d1: lambda e: e.dma_start(
                    hp[idx % 2][:, d0 * TP:d1 * TP].rearrange("p (a c) -> p a c", a=d1 - d0),
                    hT_scr.ap()[d0 * 128:d1 * 128, tp * TP:(tp + 1) * TP]
                    .rearrange("(a p) c -> p a c", p=128)))(idx, tp, d0, d1),
                    waits=[(s_qm, hp_free.get(idx - 2, 0))] if spl == 0 else [],
                    incs=[(s_hp[idx % 2], 16)])

        def qkv_piece(g, tp):
            idx = g * NTP + tp
            par = g % 2
            wv_ = 16 * NSPL * (g // 2 + 1)
            hp_need = 16 * NSPL * (idx // 2 + 1)
            groups = [("k", h) for h in range(HG)] + [("v", h) for h in range(HG)]
            if tp < NOWNP:
                groups += [("q", h) for h in range(HG)]
            for gi, (kind, hj) in enumerate(groups):
                bank = (state["bank"] + 1) % 2
                state["bank"] = bank
                wsel = {"k": wk_b, "v": wv_b, "q": wq_b}[kind]
                for dc in range(DC):
                    fn = (lambda idx, hj, dc, bank, wsel: lambda e: e.matmul(
                        qk_ps[bank][:, 0:TP],
                        wsel[:, (dc * HG + hj) * 128:(dc * HG + hj + 1) * 128],
                        hp[idx % 2][:, dc * TP:(dc + 1) * TP],
                        start=(dc == 0), stop=(dc == DC - 1)))(idx, hj, dc, bank, wsel)
                    waits = []
                    if dc == 0:
                        waits = [(s_qe, qk_user[bank]), (s_hp[idx % 2], hp_need),
                                 (s_w[0][g % 2], wv_), (s_w[1][g % 2], wv_), (s_w[2][g % 2], wv_)]
                    incs = [(s_qm, 1)] if dc == DC - 1 else []
                    pB.op("pe", fn, waits=waits, incs=incs)
                if gi == len(groups) - 1:
                    hp_free[idx] = s_qm.n
                if kind == "q" and hj == HG - 1 and tp == NOWNP - 1:
                    w_free[(0, g)] = s_qm.n
                if kind == "k" and hj == HG - 1 and tp == NTP - 1:
                    w_free[(1, g)] = s_qm.n
                if kind == "v" and hj == HG - 1 and tp == NTP - 1:
                    w_free[(2, g)] = s_qm.n
                qmv = s_qm.n
                if kind == "k":
                    pB.op("dve", (lambda par, hj, tp, bank: lambda e: e.tensor_copy(
                        KT[par][hj][:, tp * TP:(tp + 1) * TP], qk_ps[bank][:, 0:TP]))(par, hj, tp, bank),
                        waits=[(s_qm, qmv)], incs=[(s_qe, 1)])
                elif kind == "q":
                    pB.op("dve", (lambda par, hj, tp, bank: lambda e: e.tensor_scalar(
                        QT[par][hj][:, tp * TP:(tp + 1) * TP], qk_ps[bank][:, 0:TP],
                        SCALE, None, ALU.mult))(par, hj, tp, bank),
                        waits=[(s_qm, qmv)], incs=[(s_qe, 1)])
                else:
                    vp = (idx * HG + hj) % 2
                    pB.op("dve", (lambda vp, bank: lambda e: e.tensor_copy(
                        vstg[vp][:, :], qk_ps[bank][:, 0:TP]))(vp, bank),
                        waits=[(s_qm, qmv), (s_vw[vp], vw_cnt[vp])],
                        incs=[(s_qe, 1)])
                    qev = s_qe.n
                    pB.op("sp", (lambda g, hj, tp, vp: lambda e: e.dma_start(
                        VT_scr.ap()[(g * HG + hj) * 128:(g * HG + hj + 1) * 128,
                                    tp * TP:(tp + 1) * TP],
                        vstg[vp][:, :]))(g, hj, tp, vp),
                        waits=[(s_qe, qev)], incs=[(s_vw[vp], 16)])
                    vw_cnt[vp] += 16
                qk_user[bank] = s_qe.n

        def emit_vtrans(g):
            par = g % 2
            first = True
            for hj in range(HG):
                for kt in range(NKT):
                    waits = []
                    if first:
                        waits = [(s_vw[0], vw_cnt[0]), (s_vw[1], vw_cnt[1]),
                                 (s_av, max(0, (g - 1) * UPG))]
                        first = False
                    pB.op("sp", (lambda g, hj, kt, par: lambda e: e.dma_start(
                        VA[par][hj][:, kt * 128:(kt + 1) * 128],
                        VT_scr.ap()[(g * HG + hj) * 128:(g * HG + hj + 1) * 128,
                                    kt * 128:(kt + 1) * 128],
                        transpose=True))(g, hj, kt, par),
                        waits=waits, incs=[(s_vt, 16)])
            vt_done[g] = s_vt.n

        def attn_unit(g, hh, qc):
            u = state["unit"]
            state["unit"] = u + 1
            par = g % 2
            bias_ap = zeros_col if (u < 2 or ZB) else biasr[u % 4]
            for ktg in range(NKTG):
                buf = ktg % 2
                for j in range(KTG):
                    kt = ktg * KTG + j
                    waits = []
                    if j == 0:
                        waits = [(s_ex, st_user[buf][0]), (s_dn, st_user[buf][1])]
                        if ktg == 0:
                            waits += [(s_qe, qe_step_done[g]), (s_vt, vt_done[g])]
                    pB.op("pe", (lambda par, hh, qc, kt, j, buf: lambda e: e.matmul(
                        st_ps[buf][:, j * QC:(j + 1) * QC],
                        KT[par][hh][:, kt * 128:(kt + 1) * 128],
                        QT[par][hh][:, qc * QC:(qc + 1) * QC],
                        start=True, stop=True))(par, hh, qc, kt, j, buf),
                        waits=waits, incs=[(s_stm, 1)] if j == KTG - 1 else [])
                stv = s_stm.n
                ewaits = [(s_stm, stv), (s_dacc, at_guard[ktg % 2])]
                if ktg == 0:
                    ewaits += [(s_av, u)]
                    if u >= 2 and not ZB:
                        ewaits += [(s_br, s_br.n)]
                elif ktg >= 2:
                    ewaits += [(s_avk, (NKTG - 1) * u + ktg - 1)]
                pB.op("act", (lambda buf, ktg, bias_ap: lambda e: e.activation(
                    AT[ktg % 2][:, :], st_ps[buf][:, 0:KTG * QC], AF.Exp,
                    bias=bias_ap[:, 0:1], scale=1.0))(buf, ktg, bias_ap),
                    waits=ewaits, incs=[(s_ex, 1)])
                if ktg == NKTG - 1:
                    pB.op("act", lambda e: e.nop(), incs=[(s_eu, 1)])
                st_user[buf] = (s_ex.n, st_user[buf][1])
                exv = s_ex.n
                if ktg == 0:
                    pB.op("dve", lambda e: e.tensor_copy(acc[:, :], AT[0][:, :]),
                          waits=[(s_ex, exv), (s_dacc, s_dacc.n)], incs=[(s_dacc, 1)])
                else:
                    pB.op("dve", (lambda ktg: lambda e: e.tensor_add(
                        acc[:, :], acc[:, :], AT[ktg % 2][:, :]))(ktg),
                        waits=[(s_ex, exv), (s_dacc, s_dacc.n)], incs=[(s_dacc, 1)])
                at_guard[ktg % 2] = s_dacc.n
                for j in range(KTG):
                    kt = ktg * KTG + j
                    waits = []
                    if j == 0:
                        waits = [(s_ex, exv)]
                        if ktg == 0:
                            waits += [(s_oe, max(0, u - 1))]
                    incs = []
                    if kt == NKT - 1:
                        incs = [(s_av, 1)]
                    elif j == KTG - 1 and ktg < NKTG - 1:
                        incs = [(s_avk, 1)]
                    pB.op("pe", (lambda par, hh, qc, kt, j, ktg, u: lambda e: e.matmul(
                        ov_ps[u % 2][:, 0:QC],
                        VA[par][hh][:, kt * 128:(kt + 1) * 128],
                        AT[ktg % 2][:, j * QC:(j + 1) * QC],
                        start=(kt == 0), stop=(kt == NKT - 1)))(par, hh, qc, kt, j, ktg, u),
                        waits=waits, incs=incs)
            fv = s_dacc.n
            pB.op("pe", lambda e: e.matmul(
                st_ps[1][0:1, 0:QC], ones_col_f[:, :], acc[:, 0:QC],
                start=True, stop=False),
                waits=[(s_dacc, fv), (s_ex, st_user[1][0]), (s_dn, st_user[1][1])])
            pB.op("pe", lambda e: e.matmul(
                st_ps[1][0:1, 0:QC], ones_col_f[:, :], acc[:, QC:2 * QC],
                start=False, stop=True),
                incs=[(s_tp, 1)])
            denv = s_tp.n
            pB.op("dve", lambda e: e.reciprocal(acc[0:1, 0:QC], st_ps[1][0:1, 0:QC]),
                  waits=[(s_tp, denv)], incs=[(s_dn, 1)])
            rv_ = s_dn.n
            if not ZB:
                pB.op("dve", lambda e: e.reduce_max(
                    dmx[0:1, 0:1], st_ps[1][0:1, 0:QC], AX.X),
                    waits=[(s_tp, denv)], incs=[(s_dn, 1)])
                rmv = s_dn.n
                pB.op("act", lambda e: e.activation(
                    lnm[0:1, 0:1], dmx[0:1, 0:1], AF.Ln, bias=0.0, scale=1.0),
                    waits=[(s_dn, rmv)], incs=[(s_gm, 1)])
                gv1 = s_gm.n
                prev = zeros_col[0:1, 0:1] if u < 2 else \
                    negm[0:1, (u - 2) % 4:(u - 2) % 4 + 1]
                pB.op("dve", (lambda un, prev: lambda e: e.tensor_tensor(
                    negm[0:1, un:un + 1], prev, lnm[0:1, 0:1], ALU.subtract))(u % 4, prev),
                    waits=[(s_gm, gv1), (s_bw, 16 * max(0, u - 3))], incs=[(s_gm, 1)])
                gv2 = s_gm.n
                pB.op("act", (lambda u, un: lambda e: e.dma_start(
                    bias_scr.ap()[u:u + 1, 0:1], negm[0:1, un:un + 1]))(u, u % 4),
                    waits=[(s_gm, gv2)], incs=[(s_bw, 16)])
                bwv = s_bw.n
                pB.op("act", (lambda u: lambda e: e.dma_start(
                    biasr[(u + 2) % 4][:, 0:1], bass.AP(bias_scr, u * 128, [[0, 128], [1, 1]])))(u),
                    waits=[(s_bw, bwv), (s_eu, max(0, u - 1))], incs=[(s_br, 16)])
            pB.op("pe", lambda e: e.matmul(
                st_ps[1][:, QC:2 * QC], ones_row_f[0:1, :], acc[0:1, 0:QC],
                start=True, stop=True),
                waits=[(s_dn, rv_)], incs=[(s_tp, 1)])
            pB.op("dve", lambda e: e.tensor_copy(rdn_bc[:, :], st_ps[1][:, QC:2 * QC]),
                  waits=[(s_tp, s_tp.n)], incs=[(s_dn, 1)])
            st_user[1] = (st_user[1][0], s_dn.n)
            bcv = s_dn.n
            pB.op("dve", (lambda u: lambda e: e.tensor_mul(
                ostg[u % 2][:, :], ov_ps[u % 2][:, 0:QC], rdn_bc[:, :]))(u),
                waits=[(s_dn, bcv), (s_av, u + 1), (s_ob[u % 2], 16 * (u // 2))],
                incs=[(s_oe, 1)])
            oev = s_oe.n
            pB.op("sp", (lambda g, hh, qc, u: lambda e: e.dma_start(
                O_scr.ap()[(g * HG + hh) * 128:(g * HG + hh + 1) * 128,
                           qc * QC:(qc + 1) * QC],
                ostg[u % 2][:, :]))(g, hh, qc, u),
                waits=[(s_oe, oev)], incs=[(s_ob[u % 2], 16)])

        NSTR = 16
        RS = D // NSTR
        pcast = [(WoT, Wo_scr, r_) for r_ in range(NSTR)] + \
                [(WfT, Wf_scr, r_) for r_ in range(NSTR)]
        pci = {"i": 0}

        def emit_precast32(n):
            for _ in range(n):
                i = pci["i"]
                if i >= len(pcast):
                    return
                wsrc, wdst, r_ = pcast[i]
                pB.op("pool", (lambda wsrc, wdst, r_: lambda e: e.dma_start(
                    wdst.ap()[r_ * RS:(r_ + 1) * RS, :],
                    wsrc.ap()[r_ * RS:(r_ + 1) * RS, :]))(wsrc, wdst, r_),
                    incs=[(s_w32, 16)])
                pci["i"] = i + 1

        load_w(0, 0); load_w(0, 1); load_w(0, 2)
        load_piece(0, 0)
        if NTP > 1:
            load_piece(0, 1)
        for g in range(NG + 1):
            for tp in range(NTP):
                if g < NG:
                    qkv_piece(g, tp)
                    nxt = g * NTP + tp + 2
                    if nxt < NG * NTP:
                        load_piece(nxt // NTP, nxt % NTP)
                    if g + 1 < NG and tp == min(NOWNP, NTP - 1):
                        load_w(g + 1, 0)
                if g >= 1 and not NOATTN:
                    for k in range(max(1, UPG // NTP)):
                        uu = tp * max(1, UPG // NTP) + k
                        if uu < UPG:
                            attn_unit(g - 1, uu % HG, uu // HG)
            if g < NG:
                qe_step_done[g] = s_qe.n
                emit_vtrans(g)
                emit_precast32(4)
                if g + 1 < NG:
                    load_w(g + 1, 1)
                    load_w(g + 1, 2)
        pB.op("dve", lambda e: e.drain(), incs=[(s_oe, 1)])
        emit_precast32(len(pcast))
        endw = [(s_ob[i_], s_ob[i_].n) for i_ in range(2)]
        endw += [(s_w32, s_w32.n)]
        endw += [(s_br, s_br.n), (s_bw, s_bw.n),
                (s_vw[0], s_vw[0].n), (s_vw[1], s_vw[1].n), (s_vt, s_vt.n)]
        endw += [(s_hp[i_], s_hp[i_].n) for i_ in range(2)]
        endw += [(s_w[p_][q_], s_w[p_][q_].n) for p_ in range(3) for q_ in range(2)]
        pB.op("sp", lambda e: e.nop(), waits=endw)

        with nc.Block() as blkB:
            pB.emit(blkB)
    if "D" not in PHASES:
        _finish_debug(nc, out_h, D, TOWN)
        es.close()
        return nc

    with ExitStack() as esD:
        sbD = lambda n, s, d: esD.enter_context(nc.sbuf_tensor(n, s, d))
        psD = lambda n, s, d: esD.enter_context(nc.psum_tensor(n, _pad_psum(s, d), d))
        x1T = sbD("x1T", [128, DC * TC], F32)
        h2T = sbD("h2T", [128, DC * TC], BF16)
        OT = sbD("OT", [128, DC * TC], BF16)
        NWOB = DC if cfg.get("DBG4") else 2
        wo_b = [sbD(f"wo_b{i}", [128, DC * 128], BF16) for i in range(NWOB)]
        wf_b = wo_b
        as_b = sbD("as_b", [128, DC * ER], BF16)
        bf_b = sbD("bf_b", [128, ERC * D], BF16)
        xsl = [sbD(f"xsl{i}", [128, TC], F32) for i in range(2)]
        x1b = [sbD("x1b0", [128, TC], BF16)]
        xq2 = [sbD("xq20", [128, TC], BF16)]
        t1s = [sbD(f"t1s{i}", [128, TC], BF16) for i in range(ERC)]
        NOUT = 3
        outb = [sbD(f"outb{i}", [128, TC], F32) for i in range(NOUT)]
        psb = [sbD(f"psb{i}", [128, TC], F32) for i in range(2)]
        m_bc = sbD("m_bc", [128, TC], F32)
        r_bc = sbD("r_bc", [128, TC], F32)
        rows = sbD("rows", [1, 4 * TC], F32)
        biat = sbD("biat", [128, DC], F32)
        wlo = sbD("wlo", [16, 1], F32)
        whi = sbD("whi", [8, 1], F32)
        slo = sbD("slo", [16, 1], F32)
        shi = sbD("shi", [8, 1], F32)
        E16 = sbD("E16", [16, 128], F32)
        wex = [sbD(f"wex{i}", [128, 1], F32) for i in range(ERC)]

        p_ps = [psD(f"p_ps{i}", [128, TC], F32) for i in range(2)]
        f_ps = [psD(f"f_ps{i}", [128, TC], F32) for i in range(2)]
        sxD = psD("sxD", [1, TC], F32)
        sx2D = psD("sx2D", [1, TC], F32)
        u_ps = psD("u_ps", [128, TC], F32)
        u2_ps = psD("u2_ps", [128, TC], F32)

        s_ot = sem("s_ot"); s_wo = [sem("s_wo0"), sem("s_wo1")]
        s_xs = [sem("s_xs0"), sem("s_xs1")]
        s_pm = sem("s_pm"); s_pe2 = sem("s_pe2"); s_db = sem("s_db"); s_sx2 = sem("s_sx2")
        s_rw = sem("s_rw"); s_bc2 = sem("s_bc2"); s_be2 = sem("s_be2"); s_h2 = sem("s_h2")
        s_t1 = sem("s_t1"); s_t1e = sem("s_t1e"); s_fm = sem("s_fm"); s_fe = sem("s_fe")
        s_out = [sem(f"s_out{i_}") for i_ in range(4)]
        s_pc = sem("s_pc"); s_fc = sem("s_fc")
        s_wx = sem("s_wx"); s_io = sem("s_io"); s_wem = sem("s_wem"); s_wee = sem("s_wee")

        pD = Prog()
        NSPL = max(1, DC // 8)
        SPW = DC // NSPL

        for spl in range(NSPL):
            d0, d1 = spl * SPW, (spl + 1) * SPW
            pD.op("pool", (lambda d0, d1: lambda e: e.dma_start(
                as_b[:, d0 * ER:d1 * ER].rearrange("p (a c) -> p a c", a=d1 - d0),
                AsT.ap()[d0 * 128:d1 * 128, :].rearrange("(a p) c -> p a c", p=128)))(d0, d1),
                incs=[(s_wx, 16)])
        pD.op("pool", lambda e: e.dma_start(
            bf_b[:, :].rearrange("p (a c) -> p a c", a=ERC),
            BfL.ap().rearrange("(a p) c -> p a c", p=128)), incs=[(s_wx, 16)])
        pD.op("pool", lambda e: e.dma_start(biat[:, :], bias_h.ap()), incs=[(s_wx, 16)])
        pD.op("pool", lambda e: e.dma_start(wlo[:, :], cs_h.ap()[0:16, 0:1], allow_slow_non_contiguous=True),
              waits=[(s_wx, s_wx.n)], incs=[(s_wx, 16)])
        pD.op("pool", lambda e: e.dma_start(whi[:, :], cs_h.ap()[8:16, 0:1], allow_slow_non_contiguous=True),
              waits=[(s_wx, s_wx.n)], incs=[(s_wx, 16)])
        pD.op("pool", lambda e: e.dma_start(slo[:, :], cs_h.ap()[0:16, 1:2], allow_slow_non_contiguous=True),
              waits=[(s_wx, s_wx.n)], incs=[(s_wx, 16)])
        pD.op("pool", lambda e: e.dma_start(shi[:, :], cs_h.ap()[8:16, 1:2], allow_slow_non_contiguous=True),
              waits=[(s_wx, s_wx.n)], incs=[(s_wx, 16)])
        wxv = s_wx.n
        Ea = psb[0][0:16, 0:128]
        Eb = psb[1][0:16, 0:128]
        pD.op("pool", lambda e: e.iota(Ea, [[1, 128]], base=0, channel_multiplier=-16,
                                       allow_small_or_imprecise_dtypes=True), incs=[(s_io, 1)])
        pD.op("dve", lambda e: e.tensor_scalar(E16[:, :], Ea, 0.0, None, ALU.is_ge),
              waits=[(s_wx, wxv), (s_io, 1)], incs=[(s_rw, 1)])
        pD.op("dve", lambda e: e.tensor_scalar(Eb, Ea, 15.5, None, ALU.is_le),
              waits=[(s_rw, s_rw.n)], incs=[(s_rw, 1)])
        pD.op("dve", lambda e: e.tensor_mul(E16[:, :], E16[:, :], Eb),
              waits=[(s_rw, s_rw.n)], incs=[(s_rw, 1)])
        pD.op("dve", lambda e: e.tensor_mul(wlo[:, :], wlo[:, :], slo[:, :]),
              waits=[(s_wx, wxv)], incs=[(s_rw, 1)])
        pD.op("dve", lambda e: e.tensor_mul(whi[:, :], whi[:, :], shi[:, :]), incs=[(s_rw, 1)])
        rv = s_rw.n
        pD.op("pe", lambda e: e.matmul(u_ps[:, 0:1], E16[:, :], wlo[:, 0:1], start=True, stop=True),
              waits=[(s_rw, rv)], incs=[(s_wem, 1)])
        pD.op("pe", lambda e: e.matmul(u2_ps[:, 0:1], E16[0:8, :], whi[:, 0:1], start=True, stop=True),
              incs=[(s_wem, 1)])
        pD.op("dve", lambda e: e.tensor_copy(wex[0][:, :], u_ps[:, 0:1]),
              waits=[(s_wem, 2)], incs=[(s_wee, 1)])
        pD.op("dve", lambda e: e.tensor_copy(wex[1][:, :], u2_ps[:, 0:1]), incs=[(s_wee, 1)])

        p_user = [0, 0]
        f_user = [0, 0]
        pc_user = [0, 0]
        fc_user = [0, 0]

        sx_free2 = 0
        prev_fe = 0
        for tc in range(NTC):
            for dhc in range(DC):
                pD.op("sp", (lambda tc, dhc: lambda e: e.dma_start(
                    OT[:, dhc * TC:(dhc + 1) * TC],
                    O_scr.ap()[dhc * 128:(dhc + 1) * 128, tc * TC:(tc + 1) * TC]))(tc, dhc),
                    waits=[(s_pm, tc * DC)] if dhc == 0 else [], incs=[(s_ot, 16)])
            otv = s_ot.n
            if DBG2 and tc == 0:
                for dhc in range(DC):
                    pD.op("sp", (lambda dhc: lambda e: e.dma_start(
                        ot_dbg.ap()[dhc * 128:(dhc + 1) * 128, :],
                        OT[:, dhc * TC:(dhc + 1) * TC]))(dhc),
                        waits=[(s_ot, otv)] if dhc == 0 else [], incs=[(s_out[1], 16)])
            for oc in range(DC):
                wo_ord = tc * DC + oc
                uni = tc * 2 * DC + oc
                pD.op("pool", (lambda oc, par=oc % 2: lambda e: e.dma_start(
                    wo_b[oc % NWOB][:, :].rearrange("p (a c) -> p a c", a=DC),
                    Wo_scr.ap()[:, oc * 128:(oc + 1) * 128].rearrange("(a p) c -> p a c", p=128)))(oc),
                    waits=[(s_pm, max(0, wo_ord - 1)), (s_fm, tc * DC)],
                    incs=[(s_wo[oc % 2], 16)])
                pD.op("pool", (lambda tc, oc: lambda e: e.dma_start(
                    xsl[oc % 2][:, :], xT.ap()[oc * 128:(oc + 1) * 128, tc * TC:(tc + 1) * TC]))(tc, oc),
                    waits=[(s_pe2, p_user[oc % 2])], incs=[(s_xs[oc % 2], 16)])
                bank = oc % 2
                for dhc in range(DC):
                    waits = []
                    if dhc == 0:
                        waits = [(s_ot, otv), (s_wo[oc % 2], 16 * (uni // 2 + 1)),
                                 (s_pc, pc_user[bank])]
                    pD.op("pe", (lambda oc, dhc, bank: lambda e: e.matmul(
                        p_ps[bank][:, 0:TC], wo_b[oc % NWOB][:, dhc * 128:(dhc + 1) * 128],
                        OT[:, dhc * TC:(dhc + 1) * TC],
                        start=(dhc == 0), stop=(dhc == DC - 1)))(oc, dhc, bank),
                        waits=waits,
                        incs=([(s_pm, 1)] if dhc == DC - 1 else []))
                pmv = s_pm.n
                if DBG2 and tc == 0 and oc == 2:
                    pD.op("act", (lambda bank: lambda e: e.copy(
                        outb[2][:, :], p_ps[bank][:, 0:TC]))(bank),
                        waits=[(s_pm, pmv)], incs=[(s_io, 1)])
                    pD.op("act", lambda e: e.dma_start(ps_dbg.ap(), outb[2][:, :]),
                        waits=[(s_io, s_io.n)], incs=[(s_out[1], 16)])
                pD.op("act", (lambda oc, bank: lambda e: e.copy(
                    psb[oc % 2][:, :], p_ps[bank][:, 0:TC]))(oc, bank),
                    waits=[(s_pm, pmv), (s_pe2, p_user[oc % 2])] +
                          ([(s_fe, prev_fe)] if oc < 2 else []),
                    incs=[(s_pc, 1)])
                pcv = s_pc.n
                if DBG2 and tc == 0 and oc == 2:
                    pD.op("sp", lambda e: e.dma_start(xs_dbg.ap(), xsl[0][:, :]),
                          waits=[(s_xs[0], 32)], incs=[(s_out[2], 16)])
                pD.op("dve", (lambda oc: lambda e: e.tensor_add(
                    x1T[:, oc * TC:(oc + 1) * TC], psb[oc % 2][:, :], xsl[oc % 2][:, :]))(oc),
                    waits=[(s_pc, pcv), (s_xs[oc % 2], 16 * ((tc * DC + oc) // 2 + 1))],
                    incs=[(s_pe2, 1)])
                p_user[bank] = s_pe2.n
                pc_user[bank] = pcv
                gi = tc * DC + oc
                pD.op("dve", (lambda oc: lambda e: e.tensor_copy(
                    x1b[0][:, :], x1T[:, oc * TC:(oc + 1) * TC]))(oc),
                    waits=([(s_sx2, 2 * gi)] if gi >= 1 else []) + [(s_pe2, s_pe2.n)],
                    incs=[(s_db, 1)])
                pD.op("dve", (lambda oc: lambda e: e.tensor_mul(
                    xq2[0][:, :], x1b[0][:, :], x1b[0][:, :]))(oc),
                    waits=[(s_db, s_db.n)], incs=[(s_db, 1)])
                dbv = s_db.n
                pD.op("pe", (lambda oc: lambda e: e.matmul(
                    sxD[:, 0:TC], ones_col_bf[:, :], x1b[0][:, :],
                    start=(oc == 0), stop=(oc == DC - 1)))(oc),
                    waits=[(s_db, dbv)] + ([(s_rw, sx_free2)] if oc == 0 else []),
                    incs=[(s_sx2, 1)])
                pD.op("pe", (lambda oc: lambda e: e.matmul(
                    sx2D[:, 0:TC], ones_col_bf[:, :], xq2[0][:, :],
                    start=(oc == 0), stop=(oc == DC - 1)))(oc),
                    incs=[(s_sx2, 1)])
            sxv = s_sx2.n
            m_r, v_r, r_r, t_r = [slice(i * TC, (i + 1) * TC) for i in range(4)]
            b_r = t_r
            pD.op("dve", lambda e: e.tensor_scalar(rows[0:1, m_r], sxD[:, 0:TC], 1.0 / D, None, ALU.mult),
                  waits=[(s_sx2, sxv)], incs=[(s_rw, 1)])
            pD.op("dve", lambda e: e.tensor_scalar(rows[0:1, v_r], sx2D[:, 0:TC], 1.0 / D, None, ALU.mult),
                  incs=[(s_rw, 1)])
            sx_free2 = s_rw.n
            pD.op("dve", lambda e: e.tensor_mul(rows[0:1, t_r], rows[0:1, m_r], rows[0:1, m_r]),
                  waits=[(s_rw, s_rw.n)], incs=[(s_rw, 1)])
            pD.op("dve", lambda e: e.tensor_sub(rows[0:1, v_r], rows[0:1, v_r], rows[0:1, t_r]),
                  waits=[(s_rw, s_rw.n)], incs=[(s_rw, 1)])
            rw0 = s_rw.n
            pD.op("act", lambda e: e.activation(rows[0:1, t_r], rows[0:1, v_r], AF.Ln,
                                                bias=eps_t[0:1, 0:1], scale=1.0),
                  waits=[(s_rw, rw0)], incs=[(s_rw, 1)])
            pD.op("act", lambda e: e.activation(rows[0:1, r_r], rows[0:1, t_r], AF.Exp,
                                                bias=0.0, scale=-0.5),
                  waits=[(s_rw, s_rw.n)], incs=[(s_rw, 1)])
            rw1 = s_rw.n
            pD.op("dve", lambda e: e.tensor_mul(rows[0:1, b_r], rows[0:1, m_r], rows[0:1, r_r]),
                  waits=[(s_rw, rw1)], incs=[(s_rw, 1)])
            pD.op("dve", lambda e: e.tensor_scalar(rows[0:1, b_r], rows[0:1, b_r], -1.0, None, ALU.mult),
                  waits=[(s_rw, s_rw.n)], incs=[(s_rw, 1)])
            rw2 = s_rw.n
            pD.op("pe", lambda e: e.matmul(u_ps[:, 0:TC], ones_row_f[0:1, :], rows[0:1, r_r],
                                           start=True, stop=True),
                  waits=[(s_rw, rw2), (s_t1e, 2 * tc), (s_wee, 2)], incs=[(s_bc2, 1)])
            pD.op("pe", lambda e: e.matmul(u2_ps[:, 0:TC], ones_row_f[0:1, :], rows[0:1, b_r],
                                           start=True, stop=True),
                  incs=[(s_bc2, 1)])
            bcv = s_bc2.n
            pD.op("dve", lambda e: e.tensor_copy(r_bc[:, :], u_ps[:, 0:TC]),
                  waits=[(s_bc2, bcv)], incs=[(s_be2, 1)])
            pD.op("dve", lambda e: e.tensor_copy(m_bc[:, :], u2_ps[:, 0:TC]), incs=[(s_be2, 1)])
            bev = s_be2.n
            x1done = s_pe2.n
            for dc in range(DC):
                pD.op("dve", (lambda dc: lambda e: e.tensor_mul(
                    h2T[:, dc * TC:(dc + 1) * TC], x1T[:, dc * TC:(dc + 1) * TC], r_bc[:, :]))(dc),
                    waits=[(s_be2, bev), (s_pe2, x1done)] if dc == 0 else [], incs=[(s_h2, 1)])
                pD.op("dve", (lambda dc: lambda e: e.tensor_add(
                    h2T[:, dc * TC:(dc + 1) * TC], h2T[:, dc * TC:(dc + 1) * TC], m_bc[:, :]))(dc),
                    waits=[(s_h2, s_h2.n)], incs=[(s_h2, 1)])
            h2v = s_h2.n
            for erc in range(ERC):
                tps = u_ps if erc == 0 else u2_ps
                for dc in range(DC):
                    pD.op("pe", (lambda erc, dc, tps: lambda e: e.matmul(
                        tps[:, 0:TC], as_b[:, dc * ER + erc * 128:dc * ER + (erc + 1) * 128],
                        h2T[:, dc * TC:(dc + 1) * TC],
                        start=(dc == 0), stop=(dc == DC - 1)))(erc, dc, tps),
                        waits=[(s_h2, h2v), (s_be2, bev)] if dc == 0 else [],
                        incs=[(s_t1, 1)] if dc == DC - 1 else [])
                t1v = s_t1.n
                pD.op("dve", (lambda erc, tps: lambda e: e.tensor_scalar(
                    t1s[erc][:, :], tps[:, 0:TC], wex[erc][:, 0:1], None, ALU.mult))(erc, tps),
                    waits=[(s_t1, t1v), (s_wee, 2)], incs=[(s_t1e, 1)])
            t1ev = s_t1e.n
            if DBG2:
                for dc in range(DC):
                    pD.op("sp", (lambda tc, dc: lambda e: e.dma_start(
                        x1_dbg.ap()[dc * 128:(dc + 1) * 128, tc * TC:(tc + 1) * TC],
                        x1T[:, dc * TC:(dc + 1) * TC]))(tc, dc),
                        waits=[(s_h2, h2v)] if dc == 0 else [], incs=[(s_out[0], 16)])
                    pD.op("sp", (lambda tc, dc: lambda e: e.dma_start(
                        h2_dbg.ap()[dc * 128:(dc + 1) * 128, tc * TC:(tc + 1) * TC],
                        h2T[:, dc * TC:(dc + 1) * TC]))(tc, dc), incs=[(s_out[0], 16)])
                for erc in range(ERC):
                    pD.op("sp", (lambda tc, erc: lambda e: e.dma_start(
                        t1_dbg.ap()[erc * 128:(erc + 1) * 128, tc * TC:(tc + 1) * TC],
                        t1s[erc][:, :]))(tc, erc),
                        waits=[(s_t1e, t1ev)] if erc == 0 else [], incs=[(s_out[0], 16)])
            for oc in range(DC):
                wf_ord = tc * DC + oc
                uni = tc * 2 * DC + DC + oc
                pD.op("pool", (lambda oc, par=oc % 2: lambda e: e.dma_start(
                    wf_b[oc % NWOB][:, :].rearrange("p (a c) -> p a c", a=DC),
                    Wf_scr.ap()[:, oc * 128:(oc + 1) * 128].rearrange("(a p) c -> p a c", p=128)))(oc),
                    waits=[(s_fm, max(0, wf_ord - 1)), (s_pm, (tc + 1) * DC)],
                    incs=[(s_wo[oc % 2], 16)])
                bank = oc % 2
                for dc in range(DC):
                    waits = []
                    if dc == 0:
                        waits = [(s_wo[oc % 2], 16 * (uni // 2 + 1)), (s_fc, fc_user[bank]),
                                 (s_t1e, t1ev)]
                    pD.op("pe", (lambda oc, dc, bank: lambda e: e.matmul(
                        f_ps[bank][:, 0:TC], wf_b[oc % NWOB][:, dc * 128:(dc + 1) * 128],
                        h2T[:, dc * TC:(dc + 1) * TC],
                        start=(dc == 0), stop=False))(oc, dc, bank),
                        waits=waits)
                for erc in range(ERC):
                    pD.op("pe", (lambda oc, erc, bank: lambda e: e.matmul(
                        f_ps[bank][:, 0:TC], bf_b[:, erc * D + oc * 128:erc * D + (oc + 1) * 128],
                        t1s[erc][:, :],
                        start=False, stop=(erc == ERC - 1)))(oc, erc, bank),
                        incs=[(s_fm, 1)] if erc == ERC - 1 else [])
                fmv = s_fm.n
                og = tc * DC + oc
                ow = [(s_out[og % NOUT], 16 * (og // NOUT))] if og >= NOUT else []
                pD.op("act", (lambda oc, bank: lambda e: e.copy(
                    psb[oc % 2][:, :], f_ps[bank][:, 0:TC]))(oc, bank),
                    waits=[(s_fm, fmv), (s_fe, f_user[oc % 2])] +
                          ([(s_pe2, x1done)] if oc < 2 else []),
                    incs=[(s_fc, 1)])
                fcv = s_fc.n
                pD.op("dve", (lambda oc, og: lambda e: e.scalar_tensor_tensor(
                    outb[og % NOUT][:, :], psb[oc % 2][:, :], biat[:, oc:oc + 1],
                    x1T[:, oc * TC:(oc + 1) * TC], ALU.add, ALU.add))(oc, og),
                    waits=[(s_fc, fcv)] + ow, incs=[(s_fe, 1)])
                f_user[bank] = s_fe.n
                fc_user[bank] = fcv
                if oc >= 1:
                    pD.op("sp", (lambda tc, oc, og: lambda e: e.dma_start(
                        out_h.ap()[(oc - 1) * 128:oc * 128, tc * TC:(tc + 1) * TC],
                        outb[(og - 1) % NOUT][:, :]))(tc, oc, og),
                        waits=[(s_fe, s_fe.n)], incs=[(s_out[(og - 1) % NOUT], 16)])
            pD.op("dve", lambda e: e.drain(), incs=[(s_fe, 1)])
            ogl = tc * DC + DC - 1
            pD.op("sp", (lambda tc, ogl: lambda e: e.dma_start(
                out_h.ap()[(DC - 1) * 128:DC * 128, tc * TC:(tc + 1) * TC],
                outb[ogl % NOUT][:, :]))(tc, ogl),
                waits=[(s_fe, s_fe.n)], incs=[(s_out[ogl % NOUT], 16)])
            prev_fe = s_fe.n
        pD.op("sp", lambda e: e.nop(),
              waits=[(s_out[i_], s_out[i_].n) for i_ in range(4)])

        with nc.Block() as blkD:
            pD.emit(blkD)

    es.close()
    return nc


def _prep_inputs(x, Wq, Wk, Wv, Wo, Wf, bias, A, Bf, coeffs, scales, ncores=NCORES):
    x = np.asarray(x, dtype=np.float32)
    Bn, S, D = x.shape
    ER = A.shape[0] * A.shape[1]
    WqT = np.ascontiguousarray(np.asarray(Wq, np.float32).T)
    WkT = np.ascontiguousarray(np.asarray(Wk, np.float32).T)
    WvT = np.ascontiguousarray(np.asarray(Wv, np.float32).T)
    WoT = np.ascontiguousarray(np.asarray(Wo, np.float32).T)
    WfT = np.ascontiguousarray(np.asarray(Wf, np.float32).T)
    AsT = np.ascontiguousarray(np.asarray(A, np.float32).reshape(ER, D).T)
    BfLm = np.ascontiguousarray(np.asarray(Bf, np.float32).transpose(0, 2, 1).reshape(ER, D))
    biasm = np.ascontiguousarray(np.asarray(bias, np.float32).reshape(D // 128, 128).T)
    cs = np.zeros((32, 2), np.float32)
    cs[:16, 0] = np.asarray(coeffs, np.float32)
    cs[:16, 1] = np.asarray(scales, np.float32)
    TOWN = S // 2
    in_maps = []
    for c in range(ncores):
        b = (c // 2) % Bn
        half = c % 2
        xb = np.ascontiguousarray(x[b].T)
        if half == 0:
            xr = xb
        else:
            xr = np.ascontiguousarray(np.concatenate([xb[:, TOWN:], xb[:, :TOWN]], axis=1))
        in_maps.append(dict(xT=xr, WqT=WqT, WkT=WkT, WvT=WvT, WoT=WoT, WfT=WfT,
                            AsT=AsT, BfL=BfLm, bias=biasm, coefsc=cs))
    return in_maps, (Bn, S, D, TOWN)


def kernel(L0_attn_x, L0_attn_Wq, L0_attn_Wk, L0_attn_Wv, L0_attn_Wo,
           L0_ffn_W, L0_ffn_bias, L0_ffn_A, L0_ffn_Bf, L0_ffn_coeffs, L0_ffn_scales):
    in_maps, (Bn, S, D, TOWN) = _prep_inputs(
        L0_attn_x, L0_attn_Wq, L0_attn_Wk, L0_attn_Wv, L0_attn_Wo,
        L0_ffn_W, L0_ffn_bias, L0_ffn_A, L0_ffn_Bf, L0_ffn_coeffs, L0_ffn_scales)
    nc = build_program(default_cfg())
    res = run_bass_kernel_spmd(nc, in_maps, core_ids=list(range(NCORES)))
    out = np.empty((Bn, S, D), dtype=np.float32)
    for c in range(NCORES):
        b = c // 2
        half = c % 2
        o = res.results[c]["out"]
        out[b, half * TOWN:(half + 1) * TOWN, :] = np.asarray(o).T
    return out


# version 17
# speedup vs baseline: 1.2287x; 1.2287x over previous
import math
from contextlib import ExitStack

import numpy as np

import concourse.bass as bass
import concourse.mybir as mybir
import concourse.bass_utils as _bass_utils
from concourse.bass_utils import run_bass_kernel_spmd

if not getattr(_bass_utils, "_ldw_opt_patched", False):
    _orig_run_command = _bass_utils.run_command

    def _run_command_ldw(argv, **kw):
        argv = ["--enable-ldw-opt=true" if a == "--enable-ldw-opt=false" else a
                for a in argv]
        return _orig_run_command(argv, **kw)

    _bass_utils.run_command = _run_command_ldw
    _bass_utils._ldw_opt_patched = True

F32 = mybir.dt.float32
BF16 = mybir.dt.bfloat16
AF = mybir.ActivationFunctionType
ALU = mybir.AluOpType
AX = mybir.AxisListType

NCORES = 8


def default_cfg():
    return dict(D=4096, SB=4096, TOWN=2048, H=32, HG=2, QC=512, KTG=2,
                TP=512, TCH=512, TC=512, ER=256, EPS=1e-5)


class Sem:
    def __init__(self, es, nc, name):
        self.h = es.enter_context(nc.semaphore(name))
        self.n = 0


class Prog:

    def __init__(self):
        self.streams = {k: [] for k in ("pool", "sp", "pe", "act", "dve")}

    def op(self, eng, fn, waits=(), incs=()):
        for s, a in incs:
            s.n += a
        waits = [(s, v) for (s, v) in waits if v > 0]
        self.streams[eng].append((tuple(waits), fn, tuple(incs)))

    def emit(self, blk):
        def body(name):
            def run(e):
                for waits, fn, incs in self.streams[name]:
                    for s, v in waits:
                        e.wait_ge(s.h, v)
                    inst = fn(e)
                    for s, a in incs:
                        inst = inst.then_inc(s.h, a)
            return run
        if self.streams["pool"]:
            blk.gpsimd(body("pool"))
        if self.streams["sp"]:
            blk.sync(body("sp"))
        if self.streams["pe"]:
            blk.tensor(body("pe"))
        if self.streams["act"]:
            blk.scalar(body("act"))
        if self.streams["dve"]:
            blk.vector(body("dve"))


def _finish_debug(nc, out_h, D, TOWN):
    with ExitStack() as esz:
        z = esz.enter_context(nc.sbuf_tensor("zfill", [128, TOWN], mybir.dt.float32))
        sz = esz.enter_context(nc.semaphore("szf"))
        so = esz.enter_context(nc.semaphore("sof"))
        with nc.Block() as blkz:
            pz = Prog()
            szs = Sem.__new__(Sem); szs.h = sz; szs.n = 0
            sos = Sem.__new__(Sem); sos.h = so; sos.n = 0
            pz.op("pool", lambda e: e.memset(z[:, :], 0.0), incs=[(szs, 1)])
            for dc in range(D // 128):
                pz.op("sp", (lambda dc: lambda e: e.dma_start(
                    out_h.ap()[dc * 128:(dc + 1) * 128, :], z[:, :]))(dc),
                    waits=[(szs, 1)] if dc == 0 else [], incs=[(sos, 16)])
            pz.op("sp", lambda e: e.nop(), waits=[(sos, sos.n)])
            pz.emit(blkz)


def _pad_psum(shape, dt):
    per = 2048 // mybir.dt.size(dt)
    cols = ((shape[1] + per - 1) // per) * per
    return [shape[0], cols]


def build_program(cfg):
    D = cfg["D"]; SB = cfg["SB"]; TOWN = cfg["TOWN"]; H = cfg["H"]
    HG = cfg["HG"]; QC = cfg["QC"]; KTG = cfg["KTG"]; TP = cfg["TP"]
    TCH = cfg["TCH"]; TC = cfg["TC"]; ER = cfg["ER"]; EPS = cfg["EPS"]
    TC_DBG = cfg["TC"]
    DC = D // 128
    NG = H // HG
    NTP = SB // TP
    NOWNP = TOWN // TP
    NQC = TOWN // QC
    NKT = SB // 128
    NKTG = NKT // KTG
    NTCH = SB // TCH
    NTC = TOWN // TC
    QSUB = QC // 128
    ERC = ER // 128
    VPT = TP // 128
    SCALE = 1.0 / math.sqrt(128.0)
    PHASES = cfg.get("PHASES", "ABD")
    ZB = cfg.get("ZB", False)
    NOATTN = cfg.get("NOATTN", False)
    NGL = cfg.get("NGL", 0)
    QP = cfg.get("QP", "kvq")
    NUNITS = NG * HG * NQC
    assert ERC == 2 and KTG == 2 and QSUB <= 4

    nc = bass.Bass("TRN2", target_bir_lowering=False, dynamic_dma_scratch_size=16384)

    xT = nc.dram_tensor("xT", [D, SB], F32, kind="ExternalInput")
    WqT = nc.dram_tensor("WqT", [D, D], F32, kind="ExternalInput")
    WkT = nc.dram_tensor("WkT", [D, D], F32, kind="ExternalInput")
    WvT = nc.dram_tensor("WvT", [D, D], F32, kind="ExternalInput")
    WoT = nc.dram_tensor("WoT", [D, D], F32, kind="ExternalInput")
    WfT = nc.dram_tensor("WfT", [D, D], F32, kind="ExternalInput")
    AsT = nc.dram_tensor("AsT", [D, ER], F32, kind="ExternalInput")
    BfL = nc.dram_tensor("BfL", [ER, D], F32, kind="ExternalInput")
    bias_h = nc.dram_tensor("bias", [128, D // 128], F32, kind="ExternalInput")
    cs_h = nc.dram_tensor("coefsc", [32, 2], F32, kind="ExternalInput")

    dbg = "ExternalOutput" if cfg.get("DBG") else "Internal"
    DBG2 = cfg.get("DBG2", False)
    if DBG2:
        ps_dbg = nc.dram_tensor("ps_dbg", [128, TC_DBG], F32, kind="ExternalOutput")
        xs_dbg = nc.dram_tensor("xs_dbg", [128, TC_DBG], F32, kind="ExternalOutput")
        ot_dbg = nc.dram_tensor("ot_dbg", [D, TC_DBG], BF16, kind="ExternalOutput")
        x1_dbg = nc.dram_tensor("x1_dbg", [D, TOWN], F32, kind="ExternalOutput")
        h2_dbg = nc.dram_tensor("h2_dbg", [D, TOWN], BF16, kind="ExternalOutput")
        t1_dbg = nc.dram_tensor("t1_dbg", [ER, TOWN], BF16, kind="ExternalOutput")
    hT_scr = nc.dram_tensor("hT_scr", [D, SB], BF16, kind=dbg)
    O_scr = nc.dram_tensor("O_scr", [D, TOWN], BF16, kind=dbg)
    VT_scr = nc.dram_tensor("VT_scr", [D, SB], BF16)
    Wo_scr = nc.dram_tensor("Wo_scr", [D, D], BF16, kind=("ExternalOutput" if cfg.get("DBG3") else "Internal"))
    Wf_scr = nc.dram_tensor("Wf_scr", [D, D], BF16, kind=("ExternalOutput" if cfg.get("DBG3") else "Internal"))
    bias_scr = nc.dram_tensor("bias_scr", [NUNITS + 8, 128], F32)
    out_h = nc.dram_tensor("out", [D, TOWN], F32, kind="ExternalOutput")

    es = ExitStack()
    sem = lambda name: Sem(es, nc, name)

    ones_col_bf = es.enter_context(nc.sbuf_tensor("ones_col_bf", [128, 1], BF16))
    ones_row_f = es.enter_context(nc.sbuf_tensor("ones_row_f", [1, 128], F32))
    ones_col_f = es.enter_context(nc.sbuf_tensor("ones_col_f", [128, 1], F32))
    zeros_col = es.enter_context(nc.sbuf_tensor("zeros_col", [128, 1], F32))
    eps_t = es.enter_context(nc.sbuf_tensor("eps_t", [1, 1], F32))

    with nc.Block() as blk0:
        p0 = Prog()
        s0 = sem("s0")
        p0.op("pool", lambda e: e.memset(ones_col_bf[:, :], 1.0), incs=[(s0, 1)])
        p0.op("pool", lambda e: e.memset(ones_row_f[:, :], 1.0), incs=[(s0, 1)])
        p0.op("pool", lambda e: e.memset(ones_col_f[:, :], 1.0), incs=[(s0, 1)])
        p0.op("pool", lambda e: e.memset(zeros_col[:, :], 0.0), incs=[(s0, 1)])
        p0.op("pool", lambda e: e.memset(eps_t[:, :], EPS), incs=[(s0, 1)])
        p0.op("pool", lambda e: e.nop(), waits=[(s0, 4)])
        p0.emit(blk0)

    with ExitStack() as esA:
        sbA = lambda n, s, d: esA.enter_context(nc.sbuf_tensor(n, s, d))
        psA = lambda n, s, d: esA.enter_context(nc.psum_tensor(n, _pad_psum(s, d), d))
        xbf = [sbA(f"xbf{i}", [128, DC * TCH], BF16) for i in range(2)]
        QTR = DC // 4
        xf32 = [sbA(f"xf32{i}", [128, QTR * TCH], F32) for i in range(2)]
        hbf = [sbA(f"hbf{i}", [128, DC * TCH], BF16) for i in range(2)]
        xsq = [sbA(f"xsq{i}", [128, TCH], BF16) for i in range(2)]
        a_bc = [sbA(f"a_bc{i}", [128, TCH], F32) for i in range(2)]
        b_bc = [sbA(f"b_bc{i}", [128, TCH], F32) for i in range(2)]
        rowb = sbA("rowb", [1, 5 * TCH], F32)
        sx_ps = psA("sx_ps", [1, TCH], F32)
        sx2_ps = psA("sx2_ps", [1, TCH], F32)
        bc_ps = psA("bc_ps", [128, TCH], F32)

        s_xf = [sem("s_xf0"), sem("s_xf1")]; s_xcast = sem("s_xcast")
        s_xq = sem("s_xq"); s_sx = sem("s_sx")
        s_row = sem("s_row"); s_bcm = sem("s_bcm"); s_bce = sem("s_bce")
        s_h = sem("s_h"); s_hw = sem("s_hw")

        pA = Prog()
        NSPL = max(1, DC // 8)
        SPW = DC // NSPL

        h_done = []
        sx_free = 0
        for t in range(NTCH):
            par = t % 2
            c0 = t * TCH
            xw_prev = [(s_sx, 2 * DC * (t - 1))] if t >= 2 else []
            hv_prev = [(s_h, h_done[t - 2])] if t >= 2 else []
            for q in range(4):
                Q = t * 4 + q
                pA.op("act" if Q % 2 == 0 else "sp",
                      (lambda c0, q, Q: lambda e: e.dma_start(
                          xf32[Q % 2][:, :].rearrange("p (a c) -> p a c", a=QTR),
                          xT.ap()[q * QTR * 128:(q + 1) * QTR * 128, c0:c0 + TCH]
                          .rearrange("(a p) c -> p a c", p=128)))(c0, q, Q),
                      waits=[(s_xcast, max(0, Q - 1))], incs=[(s_xf[Q % 2], 16)])
                cwaits = [(s_xf[Q % 2], 16 * (Q // 2 + 1))]
                if q == 0:
                    cwaits += xw_prev + hv_prev
                pA.op("act", (lambda par, q, Q: lambda e: e.activation(
                    xbf[par][:, q * QTR * TCH:(q + 1) * QTR * TCH], xf32[Q % 2][:, :],
                    AF.Copy, bias=0.0, scale=1.0))(par, q, Q),
                    waits=cwaits, incs=[(s_xcast, 1)])
            for dc in range(DC):
                gi = t * DC + dc
                sq = dc % 2
                xcw = 4 * t + dc // QTR + 1
                pA.op("dve", (lambda par, dc, sq, xcw: lambda e: e.tensor_mul(
                    xsq[sq][:, :], xbf[par][:, dc * TCH:(dc + 1) * TCH],
                    xbf[par][:, dc * TCH:(dc + 1) * TCH]))(par, dc, sq, xcw),
                    waits=[(s_xcast, xcw)] + ([(s_sx, 2 * (gi - 1))] if gi >= 2 else []),
                    incs=[(s_xq, 1)])
                pA.op("pe", (lambda par, dc, xcw: lambda e: e.matmul(
                    sx_ps[:, 0:TCH], ones_col_bf[:, :], xbf[par][:, dc * TCH:(dc + 1) * TCH],
                    start=(dc == 0), stop=(dc == DC - 1)))(par, dc, xcw),
                    waits=[(s_xcast, xcw)] + ([(s_row, sx_free)] if dc == 0 else []),
                    incs=[(s_sx, 1)])
                pA.op("pe", (lambda dc, sq: lambda e: e.matmul(
                    sx2_ps[:, 0:TCH], ones_col_bf[:, :], xsq[sq][:, :],
                    start=(dc == 0), stop=(dc == DC - 1)))(dc, sq),
                    waits=[(s_xq, gi + 1)],
                    incs=[(s_sx, 1)])
            smm = s_sx.n
            m_r, v_r, r_r, b_r, t_r = [slice(i * TCH, (i + 1) * TCH) for i in range(5)]
            pA.op("dve", lambda e: e.tensor_scalar(rowb[0:1, m_r], sx_ps[:, 0:TCH], 1.0 / D, None, ALU.mult),
                  waits=[(s_sx, smm)], incs=[(s_row, 1)])
            pA.op("dve", lambda e: e.tensor_scalar(rowb[0:1, v_r], sx2_ps[:, 0:TCH], 1.0 / D, None, ALU.mult),
                  incs=[(s_row, 1)])
            sx_free = s_row.n
            pA.op("dve", lambda e: e.tensor_mul(rowb[0:1, t_r], rowb[0:1, m_r], rowb[0:1, m_r]),
                  waits=[(s_row, s_row.n)], incs=[(s_row, 1)])
            pA.op("dve", lambda e: e.tensor_sub(rowb[0:1, v_r], rowb[0:1, v_r], rowb[0:1, t_r]),
                  waits=[(s_row, s_row.n)], incs=[(s_row, 1)])
            rw0 = s_row.n
            pA.op("act", lambda e: e.activation(rowb[0:1, t_r], rowb[0:1, v_r], AF.Ln,
                                                bias=eps_t[0:1, 0:1], scale=1.0),
                  waits=[(s_row, rw0)], incs=[(s_row, 1)])
            pA.op("act", lambda e: e.activation(rowb[0:1, r_r], rowb[0:1, t_r], AF.Exp,
                                                bias=0.0, scale=-0.5),
                  waits=[(s_row, s_row.n)], incs=[(s_row, 1)])
            rw1 = s_row.n
            pA.op("dve", lambda e: e.tensor_mul(rowb[0:1, b_r], rowb[0:1, m_r], rowb[0:1, r_r]),
                  waits=[(s_row, rw1)], incs=[(s_row, 1)])
            pA.op("dve", lambda e: e.tensor_scalar(rowb[0:1, b_r], rowb[0:1, b_r], -1.0, None, ALU.mult),
                  waits=[(s_row, s_row.n)], incs=[(s_row, 1)])
            rw2 = s_row.n
            pA.op("pe", lambda e: e.matmul(bc_ps[:, 0:TCH], ones_row_f[0:1, :], rowb[0:1, r_r],
                                           start=True, stop=True),
                  waits=[(s_row, rw2), (s_bce, 2 * t)], incs=[(s_bcm, 1)])
            pA.op("dve", (lambda par: lambda e: e.tensor_copy(a_bc[par][:, :], bc_ps[:, 0:TCH]))(par),
                  waits=[(s_bcm, 2 * t + 1)], incs=[(s_bce, 1)])
            pA.op("pe", lambda e: e.matmul(bc_ps[:, 0:TCH], ones_row_f[0:1, :], rowb[0:1, b_r],
                                           start=True, stop=True),
                  waits=[(s_bce, 2 * t + 1)], incs=[(s_bcm, 1)])
            pA.op("dve", (lambda par: lambda e: e.tensor_copy(b_bc[par][:, :], bc_ps[:, 0:TCH]))(par),
                  waits=[(s_bcm, 2 * t + 2)], incs=[(s_bce, 1)])
            bce = s_bce.n
            hw_guard = [(s_hw, 16 * NSPL * t)] if t >= 2 else []
            for dc in range(DC):
                pA.op("dve", (lambda par, dc: lambda e: e.tensor_mul(
                    hbf[par][:, dc * TCH:(dc + 1) * TCH], xbf[par][:, dc * TCH:(dc + 1) * TCH],
                    a_bc[par][:, :]))(par, dc),
                    waits=([(s_bce, bce)] + hw_guard) if dc == 0 else [], incs=[(s_h, 1)])
                pA.op("dve", (lambda par, dc: lambda e: e.tensor_add(
                    hbf[par][:, dc * TCH:(dc + 1) * TCH], hbf[par][:, dc * TCH:(dc + 1) * TCH],
                    b_bc[par][:, :]))(par, dc),
                    waits=[(s_h, s_h.n)], incs=[(s_h, 1)])
            pA.op("dve", lambda e: e.drain(), incs=[(s_h, 1)])
            h_done.append(s_h.n)
            for spl in range(NSPL):
                d0, d1 = spl * SPW, (spl + 1) * SPW
                pA.op("sp", (lambda par, c0, d0, d1: lambda e: e.dma_start(
                    hT_scr.ap()[d0 * 128:d1 * 128, c0:c0 + TCH].rearrange("(a p) c -> p a c", p=128),
                    hbf[par][:, d0 * TCH:d1 * TCH].rearrange("p (a c) -> p a c", a=d1 - d0)))(par, c0, d0, d1),
                    waits=[(s_h, h_done[-1])] if spl == 0 else [], incs=[(s_hw, 16)])
        pA.op("sp", lambda e: e.nop(),
              waits=[(s_hw, 16 * NSPL * NTCH), (s_xf[0], s_xf[0].n), (s_xf[1], s_xf[1].n)])

        with nc.Block() as blkA:
            pA.emit(blkA)

    if "B" not in PHASES:
        _finish_debug(nc, out_h, D, TOWN)
        es.close()
        return nc

    UPG = HG * NQC
    with ExitStack() as esB:
        sbB = lambda n, s, d: esB.enter_context(nc.sbuf_tensor(n, s, d))
        psB = lambda n, s, d: esB.enter_context(nc.psum_tensor(n, _pad_psum(s, d), d))
        hp = [sbB(f"hp{i}", [128, DC * TP], BF16) for i in range(2)]
        wq_b = sbB("wq_b", [128, DC * HG * 128], BF16)
        wk_b = sbB("wk_b", [128, DC * HG * 128], BF16)
        wv_b = sbB("wv_b", [128, DC * HG * 128], BF16)
        KT = [[sbB(f"KT{p}{h}", [128, SB], BF16) for h in range(HG)] for p in range(2)]
        QT = [[sbB(f"QT{p}{h}", [128, TOWN], BF16) for h in range(HG)] for p in range(2)]
        VA = [[sbB(f"VA{p}{h}", [128, NKT * 128], BF16) for h in range(HG)] for p in range(2)]
        AT = [sbB(f"AT{i}", [128, KTG * QC], BF16) for i in range(2)]
        acc = sbB("acc", [128, KTG * QC], F32)
        vstg = [sbB(f"vstg{i}", [128, TP], BF16) for i in range(2)]
        ostg = [sbB(f"ostg{i}", [128, QC], BF16) for i in range(2)]
        negm = sbB("negm", [1, 4], F32)
        dmx = sbB("dmx", [1, 1], F32)
        lnm = sbB("lnm", [1, 1], F32)
        biasr = [sbB(f"biasr{i}", [128, 1], F32) for i in range(4)]
        rdn_bc = sbB("rdn_bc", [128, QC], F32)

        st_ps = [psB(f"st_ps{i}", [128, KTG * QC], F32) for i in range(2)]
        qk_ps = [psB(f"qk_ps{i}", [128, TP], F32) for i in range(2)]
        ov_ps = [psB(f"ov_ps{i}", [128, QC], F32) for i in range(2)]

        s_hp = [sem("s_hp0"), sem("s_hp1")]
        s_w = [[sem(f"s_w{p_}{q_}") for q_ in range(2)] for p_ in range(3)]
        s_qm = sem("s_qm"); s_qe = sem("s_qe"); s_stm = sem("s_stm")
        s_ex = sem("s_ex"); s_eu = sem("s_eu"); s_gm = sem("s_gm"); s_bw = sem("s_bw")
        s_br = sem("s_br"); s_av = sem("s_av"); s_avk = sem("s_avk"); s_oe = sem("s_oe")
        s_ob = [sem(f"s_ob{i_}") for i_ in range(2)]
        s_vw = [sem("s_vw0"), sem("s_vw1")]; s_vt = sem("s_vt")
        s_dacc = sem("s_dacc"); s_dn = sem("s_dn"); s_tp = sem("s_tp")
        s_w32 = sem("s_w32")

        pB = Prog()
        NSPL = max(1, DC // 8)
        SPW = DC // NSPL

        st_user = [(0, 0), (0, 0)]
        at_guard = [0, 0]
        qk_user = [0, 0]
        state = {"unit": 0, "bank": 1}
        qe_step_done = {}
        vt_done = {}
        hp_free = {}
        w_free = {}
        vw_cnt = [0, 0]

        def load_w(g, which):
            wt, buf = [(WqT, wq_b), (WkT, wk_b), (WvT, wv_b)][which]
            wfv = w_free.get((which, g - 1), 0)
            for spl in range(NSPL):
                d0, d1 = spl * SPW, (spl + 1) * SPW
                pB.op("pool", (lambda wt, buf, g, d0, d1: lambda e: e.dma_start(
                    buf[:, d0 * HG * 128:d1 * HG * 128].rearrange("p (a c) -> p a c", a=d1 - d0),
                    wt.ap()[d0 * 128:d1 * 128, g * HG * 128:(g + 1) * HG * 128]
                    .rearrange("(a p) c -> p a c", p=128)))(wt, buf, g, d0, d1),
                    waits=[(s_qm, wfv)] if spl == 0 else [],
                    incs=[(s_w[which][g % 2], 16)])

        def load_piece(g, tp):
            idx = g * NTP + tp
            for spl in range(NSPL):
                d0, d1 = spl * SPW, (spl + 1) * SPW
                pB.op("sp", (lambda idx, tp, d0, d1: lambda e: e.dma_start(
                    hp[idx % 2][:, d0 * TP:d1 * TP].rearrange("p (a c) -> p a c", a=d1 - d0),
                    hT_scr.ap()[d0 * 128:d1 * 128, tp * TP:(tp + 1) * TP]
                    .rearrange("(a p) c -> p a c", p=128)))(idx, tp, d0, d1),
                    waits=[(s_qm, hp_free.get(idx - 2, 0))] if spl == 0 else [],
                    incs=[(s_hp[idx % 2], 16)])

        def qkv_piece(g, tp):
            idx = g * NTP + tp
            par = g % 2
            wv_ = 16 * NSPL * (g // 2 + 1)
            hp_need = 16 * NSPL * (idx // 2 + 1)
            groups = [("k", h) for h in range(HG)] + [("v", h) for h in range(HG)]
            if tp < NOWNP:
                groups += [("q", h) for h in range(HG)]
            for gi, (kind, hj) in enumerate(groups):
                bank = (state["bank"] + 1) % 2
                state["bank"] = bank
                wsel = {"k": wk_b, "v": wv_b, "q": wq_b}[kind]
                for dc in range(DC):
                    fn = (lambda idx, hj, dc, bank, wsel: lambda e: e.matmul(
                        qk_ps[bank][:, 0:TP],
                        wsel[:, (dc * HG + hj) * 128:(dc * HG + hj + 1) * 128],
                        hp[idx % 2][:, dc * TP:(dc + 1) * TP],
                        start=(dc == 0), stop=(dc == DC - 1)))(idx, hj, dc, bank, wsel)
                    waits = []
                    if dc == 0:
                        waits = [(s_qe, qk_user[bank]), (s_hp[idx % 2], hp_need),
                                 (s_w[0][g % 2], wv_), (s_w[1][g % 2], wv_), (s_w[2][g % 2], wv_)]
                    incs = [(s_qm, 1)] if dc == DC - 1 else []
                    pB.op("pe", fn, waits=waits, incs=incs)
                if gi == len(groups) - 1:
                    hp_free[idx] = s_qm.n
                if kind == "q" and hj == HG - 1 and tp == NOWNP - 1:
                    w_free[(0, g)] = s_qm.n
                if kind == "k" and hj == HG - 1 and tp == NTP - 1:
                    w_free[(1, g)] = s_qm.n
                if kind == "v" and hj == HG - 1 and tp == NTP - 1:
                    w_free[(2, g)] = s_qm.n
                qmv = s_qm.n
                if kind == "k":
                    pB.op("dve", (lambda par, hj, tp, bank: lambda e: e.tensor_copy(
                        KT[par][hj][:, tp * TP:(tp + 1) * TP], qk_ps[bank][:, 0:TP]))(par, hj, tp, bank),
                        waits=[(s_qm, qmv)], incs=[(s_qe, 1)])
                elif kind == "q":
                    pB.op("dve", (lambda par, hj, tp, bank: lambda e: e.tensor_scalar(
                        QT[par][hj][:, tp * TP:(tp + 1) * TP], qk_ps[bank][:, 0:TP],
                        SCALE, None, ALU.mult))(par, hj, tp, bank),
                        waits=[(s_qm, qmv)], incs=[(s_qe, 1)])
                else:
                    vp = (idx * HG + hj) % 2
                    pB.op("dve", (lambda vp, bank: lambda e: e.tensor_copy(
                        vstg[vp][:, :], qk_ps[bank][:, 0:TP]))(vp, bank),
                        waits=[(s_qm, qmv), (s_vw[vp], vw_cnt[vp])],
                        incs=[(s_qe, 1)])
                    qev = s_qe.n
                    pB.op("sp", (lambda g, hj, tp, vp: lambda e: e.dma_start(
                        VT_scr.ap()[(g * HG + hj) * 128:(g * HG + hj + 1) * 128,
                                    tp * TP:(tp + 1) * TP],
                        vstg[vp][:, :]))(g, hj, tp, vp),
                        waits=[(s_qe, qev)], incs=[(s_vw[vp], 16)])
                    vw_cnt[vp] += 16
                qk_user[bank] = s_qe.n

        def emit_vtrans(g):
            par = g % 2
            first = True
            for hj in range(HG):
                for kt in range(NKT):
                    waits = []
                    if first:
                        waits = [(s_vw[0], vw_cnt[0]), (s_vw[1], vw_cnt[1]),
                                 (s_av, max(0, (g - 1) * UPG))]
                        first = False
                    pB.op("sp", (lambda g, hj, kt, par: lambda e: e.dma_start(
                        VA[par][hj][:, kt * 128:(kt + 1) * 128],
                        VT_scr.ap()[(g * HG + hj) * 128:(g * HG + hj + 1) * 128,
                                    kt * 128:(kt + 1) * 128],
                        transpose=True))(g, hj, kt, par),
                        waits=waits, incs=[(s_vt, 16)])
            vt_done[g] = s_vt.n

        def attn_unit(g, hh, qc):
            u = state["unit"]
            state["unit"] = u + 1
            par = g % 2
            bias_ap = zeros_col if (u < 2 or ZB) else biasr[u % 4]
            exvs = {}

            def emit_av(ktg):
                for j in range(KTG):
                    kt = ktg * KTG + j
                    waits = []
                    if j == 0:
                        waits = [(s_ex, exvs[ktg])]
                        if ktg == 0:
                            waits += [(s_oe, max(0, u - 1))]
                    incs = []
                    if kt == NKT - 1:
                        incs = [(s_av, 1)]
                    elif j == KTG - 1 and ktg < NKTG - 1:
                        incs = [(s_avk, 1)]
                    pB.op("pe", (lambda par, hh, qc, kt, j, ktg, u: lambda e: e.matmul(
                        ov_ps[u % 2][:, 0:QC],
                        VA[par][hh][:, kt * 128:(kt + 1) * 128],
                        AT[ktg % 2][:, j * QC:(j + 1) * QC],
                        start=(kt == 0), stop=(kt == NKT - 1)))(par, hh, qc, kt, j, ktg, u),
                        waits=waits, incs=incs)

            for ktg in range(NKTG):
                buf = ktg % 2
                for j in range(KTG):
                    kt = ktg * KTG + j
                    waits = []
                    if j == 0:
                        waits = [(s_ex, st_user[buf][0]), (s_dn, st_user[buf][1])]
                        if ktg == 0:
                            waits += [(s_qe, qe_step_done[g]), (s_vt, vt_done[g])]
                    pB.op("pe", (lambda par, hh, qc, kt, j, buf: lambda e: e.matmul(
                        st_ps[buf][:, j * QC:(j + 1) * QC],
                        KT[par][hh][:, kt * 128:(kt + 1) * 128],
                        QT[par][hh][:, qc * QC:(qc + 1) * QC],
                        start=True, stop=True))(par, hh, qc, kt, j, buf),
                        waits=waits, incs=[(s_stm, 1)] if j == KTG - 1 else [])
                stv = s_stm.n
                ewaits = [(s_stm, stv), (s_dacc, at_guard[ktg % 2])]
                if ktg == 0:
                    ewaits += [(s_av, u)]
                    if u >= 2 and not ZB:
                        ewaits += [(s_br, s_br.n)]
                elif ktg >= 2:
                    ewaits += [(s_avk, (NKTG - 1) * u + ktg - 1)]
                pB.op("act", (lambda buf, ktg, bias_ap: lambda e: e.activation(
                    AT[ktg % 2][:, :], st_ps[buf][:, 0:KTG * QC], AF.Exp,
                    bias=bias_ap[:, 0:1], scale=1.0))(buf, ktg, bias_ap),
                    waits=ewaits, incs=[(s_ex, 1)])
                if ktg == NKTG - 1:
                    pB.op("act", lambda e: e.nop(), incs=[(s_eu, 1)])
                st_user[buf] = (s_ex.n, st_user[buf][1])
                exv = s_ex.n
                if ktg == 0:
                    pB.op("dve", lambda e: e.tensor_copy(acc[:, :], AT[0][:, :]),
                          waits=[(s_ex, exv), (s_dacc, s_dacc.n)], incs=[(s_dacc, 1)])
                else:
                    pB.op("dve", (lambda ktg: lambda e: e.tensor_add(
                        acc[:, :], acc[:, :], AT[ktg % 2][:, :]))(ktg),
                        waits=[(s_ex, exv), (s_dacc, s_dacc.n)], incs=[(s_dacc, 1)])
                at_guard[ktg % 2] = s_dacc.n
                exvs[ktg] = exv
                if ktg >= 1:
                    emit_av(ktg - 1)
            emit_av(NKTG - 1)
            fv = s_dacc.n
            pB.op("pe", lambda e: e.matmul(
                st_ps[1][0:1, 0:QC], ones_col_f[:, :], acc[:, 0:QC],
                start=True, stop=False),
                waits=[(s_dacc, fv), (s_ex, st_user[1][0]), (s_dn, st_user[1][1])])
            pB.op("pe", lambda e: e.matmul(
                st_ps[1][0:1, 0:QC], ones_col_f[:, :], acc[:, QC:2 * QC],
                start=False, stop=True),
                incs=[(s_tp, 1)])
            denv = s_tp.n
            pB.op("dve", lambda e: e.reciprocal(acc[0:1, 0:QC], st_ps[1][0:1, 0:QC]),
                  waits=[(s_tp, denv)], incs=[(s_dn, 1)])
            rv_ = s_dn.n
            if not ZB:
                pB.op("dve", lambda e: e.reduce_max(
                    dmx[0:1, 0:1], st_ps[1][0:1, 0:QC], AX.X),
                    waits=[(s_tp, denv)], incs=[(s_dn, 1)])
                rmv = s_dn.n
                pB.op("act", lambda e: e.activation(
                    lnm[0:1, 0:1], dmx[0:1, 0:1], AF.Ln, bias=0.0, scale=1.0),
                    waits=[(s_dn, rmv)], incs=[(s_gm, 1)])
                gv1 = s_gm.n
                prev = zeros_col[0:1, 0:1] if u < 2 else \
                    negm[0:1, (u - 2) % 4:(u - 2) % 4 + 1]
                pB.op("dve", (lambda un, prev: lambda e: e.tensor_tensor(
                    negm[0:1, un:un + 1], prev, lnm[0:1, 0:1], ALU.subtract))(u % 4, prev),
                    waits=[(s_gm, gv1), (s_bw, 16 * max(0, u - 3))], incs=[(s_gm, 1)])
                gv2 = s_gm.n
                pB.op("act", (lambda u, un: lambda e: e.dma_start(
                    bias_scr.ap()[u:u + 1, 0:1], negm[0:1, un:un + 1]))(u, u % 4),
                    waits=[(s_gm, gv2)], incs=[(s_bw, 16)])
                bwv = s_bw.n
                pB.op("act", (lambda u: lambda e: e.dma_start(
                    biasr[(u + 2) % 4][:, 0:1], bass.AP(bias_scr, u * 128, [[0, 128], [1, 1]])))(u),
                    waits=[(s_bw, bwv), (s_eu, max(0, u - 1))], incs=[(s_br, 16)])
            pB.op("pe", lambda e: e.matmul(
                st_ps[1][:, QC:2 * QC], ones_row_f[0:1, :], acc[0:1, 0:QC],
                start=True, stop=True),
                waits=[(s_dn, rv_)], incs=[(s_tp, 1)])
            pB.op("dve", lambda e: e.tensor_copy(rdn_bc[:, :], st_ps[1][:, QC:2 * QC]),
                  waits=[(s_tp, s_tp.n)], incs=[(s_dn, 1)])
            st_user[1] = (st_user[1][0], s_dn.n)
            bcv = s_dn.n
            pB.op("dve", (lambda u: lambda e: e.tensor_mul(
                ostg[u % 2][:, :], ov_ps[u % 2][:, 0:QC], rdn_bc[:, :]))(u),
                waits=[(s_dn, bcv), (s_av, u + 1), (s_ob[u % 2], 16 * (u // 2))],
                incs=[(s_oe, 1)])
            oev = s_oe.n
            pB.op("sp", (lambda g, hh, qc, u: lambda e: e.dma_start(
                O_scr.ap()[(g * HG + hh) * 128:(g * HG + hh + 1) * 128,
                           qc * QC:(qc + 1) * QC],
                ostg[u % 2][:, :]))(g, hh, qc, u),
                waits=[(s_oe, oev)], incs=[(s_ob[u % 2], 16)])

        NSTR = 16
        RS = D // NSTR
        pcast = [(WoT, Wo_scr, r_) for r_ in range(NSTR)] + \
                [(WfT, Wf_scr, r_) for r_ in range(NSTR)]
        pci = {"i": 0}

        def emit_precast32(n):
            for _ in range(n):
                i = pci["i"]
                if i >= len(pcast):
                    return
                wsrc, wdst, r_ = pcast[i]
                pB.op("pool", (lambda wsrc, wdst, r_: lambda e: e.dma_start(
                    wdst.ap()[r_ * RS:(r_ + 1) * RS, :],
                    wsrc.ap()[r_ * RS:(r_ + 1) * RS, :]))(wsrc, wdst, r_),
                    incs=[(s_w32, 16)])
                pci["i"] = i + 1

        load_w(0, 0); load_w(0, 1); load_w(0, 2)
        load_piece(0, 0)
        if NTP > 1:
            load_piece(0, 1)
        for g in range(NG + 1):
            for tp in range(NTP):
                if g < NG:
                    qkv_piece(g, tp)
                    nxt = g * NTP + tp + 2
                    if nxt < NG * NTP:
                        load_piece(nxt // NTP, nxt % NTP)
                    if g + 1 < NG and tp == min(NOWNP, NTP - 1):
                        load_w(g + 1, 0)
                if g >= 1 and not NOATTN:
                    for k in range(max(1, UPG // NTP)):
                        uu = tp * max(1, UPG // NTP) + k
                        if uu < UPG:
                            attn_unit(g - 1, uu % HG, uu // HG)
            if g < NG:
                qe_step_done[g] = s_qe.n
                emit_vtrans(g)
                emit_precast32(4)
                if g + 1 < NG:
                    load_w(g + 1, 1)
                    load_w(g + 1, 2)
        pB.op("dve", lambda e: e.drain(), incs=[(s_oe, 1)])
        emit_precast32(len(pcast))
        endw = [(s_ob[i_], s_ob[i_].n) for i_ in range(2)]
        endw += [(s_w32, s_w32.n)]
        endw += [(s_br, s_br.n), (s_bw, s_bw.n),
                (s_vw[0], s_vw[0].n), (s_vw[1], s_vw[1].n), (s_vt, s_vt.n)]
        endw += [(s_hp[i_], s_hp[i_].n) for i_ in range(2)]
        endw += [(s_w[p_][q_], s_w[p_][q_].n) for p_ in range(3) for q_ in range(2)]
        pB.op("sp", lambda e: e.nop(), waits=endw)

        with nc.Block() as blkB:
            pB.emit(blkB)
    if "D" not in PHASES:
        _finish_debug(nc, out_h, D, TOWN)
        es.close()
        return nc

    with ExitStack() as esD:
        sbD = lambda n, s, d: esD.enter_context(nc.sbuf_tensor(n, s, d))
        psD = lambda n, s, d: esD.enter_context(nc.psum_tensor(n, _pad_psum(s, d), d))
        x1T = sbD("x1T", [128, DC * TC], F32)
        h2T = sbD("h2T", [128, DC * TC], BF16)
        OT = sbD("OT", [128, DC * TC], BF16)
        NWOB = DC if cfg.get("DBG4") else 2
        wo_b = [sbD(f"wo_b{i}", [128, DC * 128], BF16) for i in range(NWOB)]
        wf_b = wo_b
        as_b = sbD("as_b", [128, DC * ER], BF16)
        bf_b = sbD("bf_b", [128, ERC * D], BF16)
        xsl = [sbD(f"xsl{i}", [128, TC], F32) for i in range(2)]
        x1b = [sbD("x1b0", [128, TC], BF16)]
        xq2 = [sbD("xq20", [128, TC], BF16)]
        t1s = [sbD(f"t1s{i}", [128, TC], BF16) for i in range(ERC)]
        NOUT = 3
        outb = [sbD(f"outb{i}", [128, TC], F32) for i in range(NOUT)]
        psb = [sbD(f"psb{i}", [128, TC], F32) for i in range(2)]
        m_bc = sbD("m_bc", [128, TC], F32)
        r_bc = sbD("r_bc", [128, TC], F32)
        rows = sbD("rows", [1, 4 * TC], F32)
        biat = sbD("biat", [128, DC], F32)
        wlo = sbD("wlo", [16, 1], F32)
        whi = sbD("whi", [8, 1], F32)
        slo = sbD("slo", [16, 1], F32)
        shi = sbD("shi", [8, 1], F32)
        E16 = sbD("E16", [16, 128], F32)
        wex = [sbD(f"wex{i}", [128, 1], F32) for i in range(ERC)]

        p_ps = [psD(f"p_ps{i}", [128, TC], F32) for i in range(2)]
        f_ps = [psD(f"f_ps{i}", [128, TC], F32) for i in range(2)]
        sxD = psD("sxD", [1, TC], F32)
        sx2D = psD("sx2D", [1, TC], F32)
        u_ps = psD("u_ps", [128, TC], F32)
        u2_ps = psD("u2_ps", [128, TC], F32)

        s_ot = sem("s_ot"); s_wo = [sem("s_wo0"), sem("s_wo1")]
        s_xs = [sem("s_xs0"), sem("s_xs1")]
        s_pm = sem("s_pm"); s_pe2 = sem("s_pe2"); s_db = sem("s_db"); s_sx2 = sem("s_sx2")
        s_rw = sem("s_rw"); s_bc2 = sem("s_bc2"); s_be2 = sem("s_be2"); s_h2 = sem("s_h2")
        s_t1 = sem("s_t1"); s_t1e = sem("s_t1e"); s_fm = sem("s_fm"); s_fe = sem("s_fe")
        s_out = [sem(f"s_out{i_}") for i_ in range(4)]
        s_pc = sem("s_pc"); s_fc = sem("s_fc")
        s_wx = sem("s_wx"); s_io = sem("s_io"); s_wem = sem("s_wem"); s_wee = sem("s_wee")

        pD = Prog()
        NSPL = max(1, DC // 8)
        SPW = DC // NSPL

        for spl in range(NSPL):
            d0, d1 = spl * SPW, (spl + 1) * SPW
            pD.op("pool", (lambda d0, d1: lambda e: e.dma_start(
                as_b[:, d0 * ER:d1 * ER].rearrange("p (a c) -> p a c", a=d1 - d0),
                AsT.ap()[d0 * 128:d1 * 128, :].rearrange("(a p) c -> p a c", p=128)))(d0, d1),
                incs=[(s_wx, 16)])
        pD.op("pool", lambda e: e.dma_start(
            bf_b[:, :].rearrange("p (a c) -> p a c", a=ERC),
            BfL.ap().rearrange("(a p) c -> p a c", p=128)), incs=[(s_wx, 16)])
        pD.op("pool", lambda e: e.dma_start(biat[:, :], bias_h.ap()), incs=[(s_wx, 16)])
        pD.op("pool", lambda e: e.dma_start(wlo[:, :], cs_h.ap()[0:16, 0:1], allow_slow_non_contiguous=True),
              waits=[(s_wx, s_wx.n)], incs=[(s_wx, 16)])
        pD.op("pool", lambda e: e.dma_start(whi[:, :], cs_h.ap()[8:16, 0:1], allow_slow_non_contiguous=True),
              waits=[(s_wx, s_wx.n)], incs=[(s_wx, 16)])
        pD.op("pool", lambda e: e.dma_start(slo[:, :], cs_h.ap()[0:16, 1:2], allow_slow_non_contiguous=True),
              waits=[(s_wx, s_wx.n)], incs=[(s_wx, 16)])
        pD.op("pool", lambda e: e.dma_start(shi[:, :], cs_h.ap()[8:16, 1:2], allow_slow_non_contiguous=True),
              waits=[(s_wx, s_wx.n)], incs=[(s_wx, 16)])
        wxv = s_wx.n
        Ea = psb[0][0:16, 0:128]
        Eb = psb[1][0:16, 0:128]
        pD.op("pool", lambda e: e.iota(Ea, [[1, 128]], base=0, channel_multiplier=-16,
                                       allow_small_or_imprecise_dtypes=True), incs=[(s_io, 1)])
        pD.op("dve", lambda e: e.tensor_scalar(E16[:, :], Ea, 0.0, None, ALU.is_ge),
              waits=[(s_wx, wxv), (s_io, 1)], incs=[(s_rw, 1)])
        pD.op("dve", lambda e: e.tensor_scalar(Eb, Ea, 15.5, None, ALU.is_le),
              waits=[(s_rw, s_rw.n)], incs=[(s_rw, 1)])
        pD.op("dve", lambda e: e.tensor_mul(E16[:, :], E16[:, :], Eb),
              waits=[(s_rw, s_rw.n)], incs=[(s_rw, 1)])
        pD.op("dve", lambda e: e.tensor_mul(wlo[:, :], wlo[:, :], slo[:, :]),
              waits=[(s_wx, wxv)], incs=[(s_rw, 1)])
        pD.op("dve", lambda e: e.tensor_mul(whi[:, :], whi[:, :], shi[:, :]), incs=[(s_rw, 1)])
        rv = s_rw.n
        pD.op("pe", lambda e: e.matmul(u_ps[:, 0:1], E16[:, :], wlo[:, 0:1], start=True, stop=True),
              waits=[(s_rw, rv)], incs=[(s_wem, 1)])
        pD.op("pe", lambda e: e.matmul(u2_ps[:, 0:1], E16[0:8, :], whi[:, 0:1], start=True, stop=True),
              incs=[(s_wem, 1)])
        pD.op("dve", lambda e: e.tensor_copy(wex[0][:, :], u_ps[:, 0:1]),
              waits=[(s_wem, 2)], incs=[(s_wee, 1)])
        pD.op("dve", lambda e: e.tensor_copy(wex[1][:, :], u2_ps[:, 0:1]), incs=[(s_wee, 1)])

        p_user = [0, 0]
        f_user = [0, 0]
        pc_user = [0, 0]
        fc_user = [0, 0]

        sx_free2 = 0
        prev_fe = 0
        for tc in range(NTC):
            for dhc in range(DC):
                pD.op("sp", (lambda tc, dhc: lambda e: e.dma_start(
                    OT[:, dhc * TC:(dhc + 1) * TC],
                    O_scr.ap()[dhc * 128:(dhc + 1) * 128, tc * TC:(tc + 1) * TC]))(tc, dhc),
                    waits=[(s_pm, tc * DC)] if dhc == 0 else [], incs=[(s_ot, 16)])
            otv = s_ot.n
            if DBG2 and tc == 0:
                for dhc in range(DC):
                    pD.op("sp", (lambda dhc: lambda e: e.dma_start(
                        ot_dbg.ap()[dhc * 128:(dhc + 1) * 128, :],
                        OT[:, dhc * TC:(dhc + 1) * TC]))(dhc),
                        waits=[(s_ot, otv)] if dhc == 0 else [], incs=[(s_out[1], 16)])
            for oc in range(DC):
                wo_ord = tc * DC + oc
                uni = tc * 2 * DC + oc
                pD.op("pool", (lambda oc, par=oc % 2: lambda e: e.dma_start(
                    wo_b[oc % NWOB][:, :].rearrange("p (a c) -> p a c", a=DC),
                    Wo_scr.ap()[:, oc * 128:(oc + 1) * 128].rearrange("(a p) c -> p a c", p=128)))(oc),
                    waits=[(s_pm, max(0, wo_ord - 1)), (s_fm, tc * DC)],
                    incs=[(s_wo[oc % 2], 16)])
                pD.op("pool", (lambda tc, oc: lambda e: e.dma_start(
                    xsl[oc % 2][:, :], xT.ap()[oc * 128:(oc + 1) * 128, tc * TC:(tc + 1) * TC]))(tc, oc),
                    waits=[(s_pe2, p_user[oc % 2])], incs=[(s_xs[oc % 2], 16)])
                bank = oc % 2
                for dhc in range(DC):
                    waits = []
                    if dhc == 0:
                        waits = [(s_ot, otv), (s_wo[oc % 2], 16 * (uni // 2 + 1)),
                                 (s_pc, pc_user[bank])]
                    pD.op("pe", (lambda oc, dhc, bank: lambda e: e.matmul(
                        p_ps[bank][:, 0:TC], wo_b[oc % NWOB][:, dhc * 128:(dhc + 1) * 128],
                        OT[:, dhc * TC:(dhc + 1) * TC],
                        start=(dhc == 0), stop=(dhc == DC - 1)))(oc, dhc, bank),
                        waits=waits,
                        incs=([(s_pm, 1)] if dhc == DC - 1 else []))
                pmv = s_pm.n
                if DBG2 and tc == 0 and oc == 2:
                    pD.op("act", (lambda bank: lambda e: e.copy(
                        outb[2][:, :], p_ps[bank][:, 0:TC]))(bank),
                        waits=[(s_pm, pmv)], incs=[(s_io, 1)])
                    pD.op("act", lambda e: e.dma_start(ps_dbg.ap(), outb[2][:, :]),
                        waits=[(s_io, s_io.n)], incs=[(s_out[1], 16)])
                pD.op("act", (lambda oc, bank: lambda e: e.copy(
                    psb[oc % 2][:, :], p_ps[bank][:, 0:TC]))(oc, bank),
                    waits=[(s_pm, pmv), (s_pe2, p_user[oc % 2])] +
                          ([(s_fe, prev_fe)] if oc < 2 else []),
                    incs=[(s_pc, 1)])
                pcv = s_pc.n
                if DBG2 and tc == 0 and oc == 2:
                    pD.op("sp", lambda e: e.dma_start(xs_dbg.ap(), xsl[0][:, :]),
                          waits=[(s_xs[0], 32)], incs=[(s_out[2], 16)])
                pD.op("dve", (lambda oc: lambda e: e.tensor_add(
                    x1T[:, oc * TC:(oc + 1) * TC], psb[oc % 2][:, :], xsl[oc % 2][:, :]))(oc),
                    waits=[(s_pc, pcv), (s_xs[oc % 2], 16 * ((tc * DC + oc) // 2 + 1))],
                    incs=[(s_pe2, 1)])
                p_user[bank] = s_pe2.n
                pc_user[bank] = pcv
                gi = tc * DC + oc
                pD.op("dve", (lambda oc: lambda e: e.tensor_copy(
                    x1b[0][:, :], x1T[:, oc * TC:(oc + 1) * TC]))(oc),
                    waits=([(s_sx2, 2 * gi)] if gi >= 1 else []) + [(s_pe2, s_pe2.n)],
                    incs=[(s_db, 1)])
                pD.op("dve", (lambda oc: lambda e: e.tensor_mul(
                    xq2[0][:, :], x1b[0][:, :], x1b[0][:, :]))(oc),
                    waits=[(s_db, s_db.n)], incs=[(s_db, 1)])
                dbv = s_db.n
                pD.op("pe", (lambda oc: lambda e: e.matmul(
                    sxD[:, 0:TC], ones_col_bf[:, :], x1b[0][:, :],
                    start=(oc == 0), stop=(oc == DC - 1)))(oc),
                    waits=[(s_db, dbv)] + ([(s_rw, sx_free2)] if oc == 0 else []),
                    incs=[(s_sx2, 1)])
                pD.op("pe", (lambda oc: lambda e: e.matmul(
                    sx2D[:, 0:TC], ones_col_bf[:, :], xq2[0][:, :],
                    start=(oc == 0), stop=(oc == DC - 1)))(oc),
                    incs=[(s_sx2, 1)])
            sxv = s_sx2.n
            m_r, v_r, r_r, t_r = [slice(i * TC, (i + 1) * TC) for i in range(4)]
            b_r = t_r
            pD.op("dve", lambda e: e.tensor_scalar(rows[0:1, m_r], sxD[:, 0:TC], 1.0 / D, None, ALU.mult),
                  waits=[(s_sx2, sxv)], incs=[(s_rw, 1)])
            pD.op("dve", lambda e: e.tensor_scalar(rows[0:1, v_r], sx2D[:, 0:TC], 1.0 / D, None, ALU.mult),
                  incs=[(s_rw, 1)])
            sx_free2 = s_rw.n
            pD.op("dve", lambda e: e.tensor_mul(rows[0:1, t_r], rows[0:1, m_r], rows[0:1, m_r]),
                  waits=[(s_rw, s_rw.n)], incs=[(s_rw, 1)])
            pD.op("dve", lambda e: e.tensor_sub(rows[0:1, v_r], rows[0:1, v_r], rows[0:1, t_r]),
                  waits=[(s_rw, s_rw.n)], incs=[(s_rw, 1)])
            rw0 = s_rw.n
            pD.op("act", lambda e: e.activation(rows[0:1, t_r], rows[0:1, v_r], AF.Ln,
                                                bias=eps_t[0:1, 0:1], scale=1.0),
                  waits=[(s_rw, rw0)], incs=[(s_rw, 1)])
            pD.op("act", lambda e: e.activation(rows[0:1, r_r], rows[0:1, t_r], AF.Exp,
                                                bias=0.0, scale=-0.5),
                  waits=[(s_rw, s_rw.n)], incs=[(s_rw, 1)])
            rw1 = s_rw.n
            pD.op("dve", lambda e: e.tensor_mul(rows[0:1, b_r], rows[0:1, m_r], rows[0:1, r_r]),
                  waits=[(s_rw, rw1)], incs=[(s_rw, 1)])
            pD.op("dve", lambda e: e.tensor_scalar(rows[0:1, b_r], rows[0:1, b_r], -1.0, None, ALU.mult),
                  waits=[(s_rw, s_rw.n)], incs=[(s_rw, 1)])
            rw2 = s_rw.n
            pD.op("pe", lambda e: e.matmul(u_ps[:, 0:TC], ones_row_f[0:1, :], rows[0:1, r_r],
                                           start=True, stop=True),
                  waits=[(s_rw, rw2), (s_t1e, 2 * tc), (s_wee, 2)], incs=[(s_bc2, 1)])
            pD.op("pe", lambda e: e.matmul(u2_ps[:, 0:TC], ones_row_f[0:1, :], rows[0:1, b_r],
                                           start=True, stop=True),
                  incs=[(s_bc2, 1)])
            bcv = s_bc2.n
            pD.op("dve", lambda e: e.tensor_copy(r_bc[:, :], u_ps[:, 0:TC]),
                  waits=[(s_bc2, bcv)], incs=[(s_be2, 1)])
            pD.op("dve", lambda e: e.tensor_copy(m_bc[:, :], u2_ps[:, 0:TC]), incs=[(s_be2, 1)])
            bev = s_be2.n
            x1done = s_pe2.n
            for dc in range(DC):
                pD.op("dve", (lambda dc: lambda e: e.tensor_mul(
                    h2T[:, dc * TC:(dc + 1) * TC], x1T[:, dc * TC:(dc + 1) * TC], r_bc[:, :]))(dc),
                    waits=[(s_be2, bev), (s_pe2, x1done)] if dc == 0 else [], incs=[(s_h2, 1)])
                pD.op("dve", (lambda dc: lambda e: e.tensor_add(
                    h2T[:, dc * TC:(dc + 1) * TC], h2T[:, dc * TC:(dc + 1) * TC], m_bc[:, :]))(dc),
                    waits=[(s_h2, s_h2.n)], incs=[(s_h2, 1)])
            h2v = s_h2.n
            for erc in range(ERC):
                tps = u_ps if erc == 0 else u2_ps
                for dc in range(DC):
                    pD.op("pe", (lambda erc, dc, tps: lambda e: e.matmul(
                        tps[:, 0:TC], as_b[:, dc * ER + erc * 128:dc * ER + (erc + 1) * 128],
                        h2T[:, dc * TC:(dc + 1) * TC],
                        start=(dc == 0), stop=(dc == DC - 1)))(erc, dc, tps),
                        waits=[(s_h2, h2v), (s_be2, bev)] if dc == 0 else [],
                        incs=[(s_t1, 1)] if dc == DC - 1 else [])
                t1v = s_t1.n
                pD.op("dve", (lambda erc, tps: lambda e: e.tensor_scalar(
                    t1s[erc][:, :], tps[:, 0:TC], wex[erc][:, 0:1], None, ALU.mult))(erc, tps),
                    waits=[(s_t1, t1v), (s_wee, 2)], incs=[(s_t1e, 1)])
            t1ev = s_t1e.n
            if DBG2:
                for dc in range(DC):
                    pD.op("sp", (lambda tc, dc: lambda e: e.dma_start(
                        x1_dbg.ap()[dc * 128:(dc + 1) * 128, tc * TC:(tc + 1) * TC],
                        x1T[:, dc * TC:(dc + 1) * TC]))(tc, dc),
                        waits=[(s_h2, h2v)] if dc == 0 else [], incs=[(s_out[0], 16)])
                    pD.op("sp", (lambda tc, dc: lambda e: e.dma_start(
                        h2_dbg.ap()[dc * 128:(dc + 1) * 128, tc * TC:(tc + 1) * TC],
                        h2T[:, dc * TC:(dc + 1) * TC]))(tc, dc), incs=[(s_out[0], 16)])
                for erc in range(ERC):
                    pD.op("sp", (lambda tc, erc: lambda e: e.dma_start(
                        t1_dbg.ap()[erc * 128:(erc + 1) * 128, tc * TC:(tc + 1) * TC],
                        t1s[erc][:, :]))(tc, erc),
                        waits=[(s_t1e, t1ev)] if erc == 0 else [], incs=[(s_out[0], 16)])
            for oc in range(DC):
                wf_ord = tc * DC + oc
                uni = tc * 2 * DC + DC + oc
                pD.op("pool", (lambda oc, par=oc % 2: lambda e: e.dma_start(
                    wf_b[oc % NWOB][:, :].rearrange("p (a c) -> p a c", a=DC),
                    Wf_scr.ap()[:, oc * 128:(oc + 1) * 128].rearrange("(a p) c -> p a c", p=128)))(oc),
                    waits=[(s_fm, max(0, wf_ord - 1)), (s_pm, (tc + 1) * DC)],
                    incs=[(s_wo[oc % 2], 16)])
                bank = oc % 2
                for dc in range(DC):
                    waits = []
                    if dc == 0:
                        waits = [(s_wo[oc % 2], 16 * (uni // 2 + 1)), (s_fc, fc_user[bank]),
                                 (s_t1e, t1ev)]
                    pD.op("pe", (lambda oc, dc, bank: lambda e: e.matmul(
                        f_ps[bank][:, 0:TC], wf_b[oc % NWOB][:, dc * 128:(dc + 1) * 128],
                        h2T[:, dc * TC:(dc + 1) * TC],
                        start=(dc == 0), stop=False))(oc, dc, bank),
                        waits=waits)
                for erc in range(ERC):
                    pD.op("pe", (lambda oc, erc, bank: lambda e: e.matmul(
                        f_ps[bank][:, 0:TC], bf_b[:, erc * D + oc * 128:erc * D + (oc + 1) * 128],
                        t1s[erc][:, :],
                        start=False, stop=(erc == ERC - 1)))(oc, erc, bank),
                        incs=[(s_fm, 1)] if erc == ERC - 1 else [])
                fmv = s_fm.n
                og = tc * DC + oc
                ow = [(s_out[og % NOUT], 16 * (og // NOUT))] if og >= NOUT else []
                pD.op("act", (lambda oc, bank: lambda e: e.copy(
                    psb[oc % 2][:, :], f_ps[bank][:, 0:TC]))(oc, bank),
                    waits=[(s_fm, fmv), (s_fe, f_user[oc % 2])] +
                          ([(s_pe2, x1done)] if oc < 2 else []),
                    incs=[(s_fc, 1)])
                fcv = s_fc.n
                pD.op("dve", (lambda oc, og: lambda e: e.scalar_tensor_tensor(
                    outb[og % NOUT][:, :], psb[oc % 2][:, :], biat[:, oc:oc + 1],
                    x1T[:, oc * TC:(oc + 1) * TC], ALU.add, ALU.add))(oc, og),
                    waits=[(s_fc, fcv)] + ow, incs=[(s_fe, 1)])
                f_user[bank] = s_fe.n
                fc_user[bank] = fcv
                if oc >= 1:
                    pD.op("sp", (lambda tc, oc, og: lambda e: e.dma_start(
                        out_h.ap()[(oc - 1) * 128:oc * 128, tc * TC:(tc + 1) * TC],
                        outb[(og - 1) % NOUT][:, :]))(tc, oc, og),
                        waits=[(s_fe, s_fe.n)], incs=[(s_out[(og - 1) % NOUT], 16)])
            pD.op("dve", lambda e: e.drain(), incs=[(s_fe, 1)])
            ogl = tc * DC + DC - 1
            pD.op("sp", (lambda tc, ogl: lambda e: e.dma_start(
                out_h.ap()[(DC - 1) * 128:DC * 128, tc * TC:(tc + 1) * TC],
                outb[ogl % NOUT][:, :]))(tc, ogl),
                waits=[(s_fe, s_fe.n)], incs=[(s_out[ogl % NOUT], 16)])
            prev_fe = s_fe.n
        pD.op("sp", lambda e: e.nop(),
              waits=[(s_out[i_], s_out[i_].n) for i_ in range(4)])

        with nc.Block() as blkD:
            pD.emit(blkD)

    es.close()
    return nc


def _prep_inputs(x, Wq, Wk, Wv, Wo, Wf, bias, A, Bf, coeffs, scales, ncores=NCORES):
    x = np.asarray(x, dtype=np.float32)
    Bn, S, D = x.shape
    ER = A.shape[0] * A.shape[1]
    WqT = np.ascontiguousarray(np.asarray(Wq, np.float32).T)
    WkT = np.ascontiguousarray(np.asarray(Wk, np.float32).T)
    WvT = np.ascontiguousarray(np.asarray(Wv, np.float32).T)
    WoT = np.ascontiguousarray(np.asarray(Wo, np.float32).T)
    WfT = np.ascontiguousarray(np.asarray(Wf, np.float32).T)
    AsT = np.ascontiguousarray(np.asarray(A, np.float32).reshape(ER, D).T)
    BfLm = np.ascontiguousarray(np.asarray(Bf, np.float32).transpose(0, 2, 1).reshape(ER, D))
    biasm = np.ascontiguousarray(np.asarray(bias, np.float32).reshape(D // 128, 128).T)
    cs = np.zeros((32, 2), np.float32)
    cs[:16, 0] = np.asarray(coeffs, np.float32)
    cs[:16, 1] = np.asarray(scales, np.float32)
    TOWN = S // 2
    in_maps = []
    for c in range(ncores):
        b = (c // 2) % Bn
        half = c % 2
        xb = np.ascontiguousarray(x[b].T)
        if half == 0:
            xr = xb
        else:
            xr = np.ascontiguousarray(np.concatenate([xb[:, TOWN:], xb[:, :TOWN]], axis=1))
        in_maps.append(dict(xT=xr, WqT=WqT, WkT=WkT, WvT=WvT, WoT=WoT, WfT=WfT,
                            AsT=AsT, BfL=BfLm, bias=biasm, coefsc=cs))
    return in_maps, (Bn, S, D, TOWN)


def kernel(L0_attn_x, L0_attn_Wq, L0_attn_Wk, L0_attn_Wv, L0_attn_Wo,
           L0_ffn_W, L0_ffn_bias, L0_ffn_A, L0_ffn_Bf, L0_ffn_coeffs, L0_ffn_scales):
    in_maps, (Bn, S, D, TOWN) = _prep_inputs(
        L0_attn_x, L0_attn_Wq, L0_attn_Wk, L0_attn_Wv, L0_attn_Wo,
        L0_ffn_W, L0_ffn_bias, L0_ffn_A, L0_ffn_Bf, L0_ffn_coeffs, L0_ffn_scales)
    nc = build_program(default_cfg())
    res = run_bass_kernel_spmd(nc, in_maps, core_ids=list(range(NCORES)))
    out = np.empty((Bn, S, D), dtype=np.float32)
    for c in range(NCORES):
        b = c // 2
        half = c % 2
        o = res.results[c]["out"]
        out[b, half * TOWN:(half + 1) * TOWN, :] = np.asarray(o).T
    return out


# version 18
# speedup vs baseline: 1.2993x; 1.0575x over previous
import math
from contextlib import ExitStack

import numpy as np

import concourse.bass as bass
import concourse.mybir as mybir
import concourse.bass_utils as _bass_utils
from concourse.bass_utils import run_bass_kernel_spmd

if not getattr(_bass_utils, "_ldw_opt_patched", False):
    _orig_run_command = _bass_utils.run_command

    def _run_command_ldw(argv, **kw):
        argv = ["--enable-ldw-opt=true" if a == "--enable-ldw-opt=false" else a
                for a in argv]
        return _orig_run_command(argv, **kw)

    _bass_utils.run_command = _run_command_ldw
    _bass_utils._ldw_opt_patched = True

F32 = mybir.dt.float32
BF16 = mybir.dt.bfloat16
AF = mybir.ActivationFunctionType
ALU = mybir.AluOpType
AX = mybir.AxisListType

NCORES = 8


def default_cfg():
    return dict(D=4096, SB=4096, TOWN=2048, H=32, HG=2, QC=512, KTG=2,
                TP=512, TCH=512, TC=512, ER=256, EPS=1e-5)


class Sem:
    def __init__(self, es, nc, name):
        self.h = es.enter_context(nc.semaphore(name))
        self.n = 0


class Prog:

    def __init__(self):
        self.streams = {k: [] for k in ("pool", "sp", "pe", "act", "dve")}

    def op(self, eng, fn, waits=(), incs=()):
        for s, a in incs:
            s.n += a
        waits = [(s, v) for (s, v) in waits if v > 0]
        self.streams[eng].append((tuple(waits), fn, tuple(incs)))

    def emit(self, blk):
        def body(name):
            def run(e):
                for waits, fn, incs in self.streams[name]:
                    for s, v in waits:
                        e.wait_ge(s.h, v)
                    inst = fn(e)
                    for s, a in incs:
                        inst = inst.then_inc(s.h, a)
            return run
        if self.streams["pool"]:
            blk.gpsimd(body("pool"))
        if self.streams["sp"]:
            blk.sync(body("sp"))
        if self.streams["pe"]:
            blk.tensor(body("pe"))
        if self.streams["act"]:
            blk.scalar(body("act"))
        if self.streams["dve"]:
            blk.vector(body("dve"))


def _finish_debug(nc, out_h, D, TOWN):
    with ExitStack() as esz:
        z = esz.enter_context(nc.sbuf_tensor("zfill", [128, TOWN], mybir.dt.float32))
        sz = esz.enter_context(nc.semaphore("szf"))
        so = esz.enter_context(nc.semaphore("sof"))
        with nc.Block() as blkz:
            pz = Prog()
            szs = Sem.__new__(Sem); szs.h = sz; szs.n = 0
            sos = Sem.__new__(Sem); sos.h = so; sos.n = 0
            pz.op("pool", lambda e: e.memset(z[:, :], 0.0), incs=[(szs, 1)])
            for dc in range(D // 128):
                pz.op("sp", (lambda dc: lambda e: e.dma_start(
                    out_h.ap()[dc * 128:(dc + 1) * 128, :], z[:, :]))(dc),
                    waits=[(szs, 1)] if dc == 0 else [], incs=[(sos, 16)])
            pz.op("sp", lambda e: e.nop(), waits=[(sos, sos.n)])
            pz.emit(blkz)


def _pad_psum(shape, dt):
    per = 2048 // mybir.dt.size(dt)
    cols = ((shape[1] + per - 1) // per) * per
    return [shape[0], cols]


def build_program(cfg):
    D = cfg["D"]; SB = cfg["SB"]; TOWN = cfg["TOWN"]; H = cfg["H"]
    HG = cfg["HG"]; QC = cfg["QC"]; KTG = cfg["KTG"]; TP = cfg["TP"]
    TCH = cfg["TCH"]; TC = cfg["TC"]; ER = cfg["ER"]; EPS = cfg["EPS"]
    TC_DBG = cfg["TC"]
    DC = D // 128
    NG = H // HG
    NTP = SB // TP
    NOWNP = TOWN // TP
    NQC = TOWN // QC
    NKT = SB // 128
    NKTG = NKT // KTG
    NTCH = SB // TCH
    NTC = TOWN // TC
    QSUB = QC // 128
    ERC = ER // 128
    VPT = TP // 128
    SCALE = 1.0 / math.sqrt(128.0)
    PHASES = cfg.get("PHASES", "ABD")
    ZB = cfg.get("ZB", False)
    NOATTN = cfg.get("NOATTN", False)
    NGL = cfg.get("NGL", 0)
    QP = cfg.get("QP", "kvq")
    NUNITS = NG * HG * NQC
    assert ERC == 2 and KTG == 2 and QSUB <= 4

    nc = bass.Bass("TRN2", target_bir_lowering=False, dynamic_dma_scratch_size=16384)

    xT = nc.dram_tensor("xT", [D, SB], F32, kind="ExternalInput")
    WqT = nc.dram_tensor("WqT", [D, D], F32, kind="ExternalInput")
    WkT = nc.dram_tensor("WkT", [D, D], F32, kind="ExternalInput")
    WvT = nc.dram_tensor("WvT", [D, D], F32, kind="ExternalInput")
    WoT = nc.dram_tensor("WoT", [D, D], F32, kind="ExternalInput")
    WfT = nc.dram_tensor("WfT", [D, D], F32, kind="ExternalInput")
    AsT = nc.dram_tensor("AsT", [D, ER], F32, kind="ExternalInput")
    BfL = nc.dram_tensor("BfL", [ER, D], F32, kind="ExternalInput")
    bias_h = nc.dram_tensor("bias", [128, D // 128], F32, kind="ExternalInput")
    cs_h = nc.dram_tensor("coefsc", [32, 2], F32, kind="ExternalInput")

    dbg = "ExternalOutput" if cfg.get("DBG") else "Internal"
    DBG2 = cfg.get("DBG2", False)
    if DBG2:
        ps_dbg = nc.dram_tensor("ps_dbg", [128, TC_DBG], F32, kind="ExternalOutput")
        xs_dbg = nc.dram_tensor("xs_dbg", [128, TC_DBG], F32, kind="ExternalOutput")
        ot_dbg = nc.dram_tensor("ot_dbg", [D, TC_DBG], BF16, kind="ExternalOutput")
        x1_dbg = nc.dram_tensor("x1_dbg", [D, TOWN], F32, kind="ExternalOutput")
        h2_dbg = nc.dram_tensor("h2_dbg", [D, TOWN], BF16, kind="ExternalOutput")
        t1_dbg = nc.dram_tensor("t1_dbg", [ER, TOWN], BF16, kind="ExternalOutput")
    hT_scr = nc.dram_tensor("hT_scr", [D, SB], BF16, kind=dbg)
    O_scr = nc.dram_tensor("O_scr", [D, TOWN], BF16, kind=dbg)
    VT_scr = nc.dram_tensor("VT_scr", [D, SB], BF16)
    Wo_scr = nc.dram_tensor("Wo_scr", [D, D], BF16, kind=("ExternalOutput" if cfg.get("DBG3") else "Internal"))
    Wf_scr = nc.dram_tensor("Wf_scr", [D, D], BF16, kind=("ExternalOutput" if cfg.get("DBG3") else "Internal"))
    bias_scr = nc.dram_tensor("bias_scr", [NUNITS + 8, 128], F32)
    out_h = nc.dram_tensor("out", [D, TOWN], F32, kind="ExternalOutput")

    es = ExitStack()
    sem = lambda name: Sem(es, nc, name)

    ones_col_bf = es.enter_context(nc.sbuf_tensor("ones_col_bf", [128, 1], BF16))
    ones_row_f = es.enter_context(nc.sbuf_tensor("ones_row_f", [1, 128], F32))
    ones_col_f = es.enter_context(nc.sbuf_tensor("ones_col_f", [128, 1], F32))
    zeros_col = es.enter_context(nc.sbuf_tensor("zeros_col", [128, 1], F32))
    eps_t = es.enter_context(nc.sbuf_tensor("eps_t", [1, 1], F32))

    with nc.Block() as blk0:
        p0 = Prog()
        s0 = sem("s0")
        p0.op("pool", lambda e: e.memset(ones_col_bf[:, :], 1.0), incs=[(s0, 1)])
        p0.op("pool", lambda e: e.memset(ones_row_f[:, :], 1.0), incs=[(s0, 1)])
        p0.op("pool", lambda e: e.memset(ones_col_f[:, :], 1.0), incs=[(s0, 1)])
        p0.op("pool", lambda e: e.memset(zeros_col[:, :], 0.0), incs=[(s0, 1)])
        p0.op("pool", lambda e: e.memset(eps_t[:, :], EPS), incs=[(s0, 1)])
        p0.op("pool", lambda e: e.nop(), waits=[(s0, 4)])
        p0.emit(blk0)

    with ExitStack() as esA:
        sbA = lambda n, s, d: esA.enter_context(nc.sbuf_tensor(n, s, d))
        psA = lambda n, s, d: esA.enter_context(nc.psum_tensor(n, _pad_psum(s, d), d))
        xbf = [sbA(f"xbf{i}", [128, DC * TCH], BF16) for i in range(2)]
        QTR = DC // 4
        xf32 = [sbA(f"xf32{i}", [128, QTR * TCH], F32) for i in range(2)]
        hbf = [sbA(f"hbf{i}", [128, DC * TCH], BF16) for i in range(2)]
        xsq = [sbA(f"xsq{i}", [128, TCH], BF16) for i in range(2)]
        a_bc = [sbA(f"a_bc{i}", [128, TCH], F32) for i in range(2)]
        b_bc = [sbA(f"b_bc{i}", [128, TCH], F32) for i in range(2)]
        rowb = sbA("rowb", [1, 5 * TCH], F32)
        sx_ps = psA("sx_ps", [1, TCH], F32)
        sx2_ps = psA("sx2_ps", [1, TCH], F32)
        bc_ps = psA("bc_ps", [128, TCH], F32)

        s_xf = [sem("s_xf0"), sem("s_xf1")]; s_xcast = sem("s_xcast")
        s_xq = sem("s_xq"); s_sx = sem("s_sx")
        s_row = sem("s_row"); s_bcm = sem("s_bcm"); s_bce = sem("s_bce")
        s_h = sem("s_h"); s_hw = sem("s_hw")

        pA = Prog()
        NSPL = max(1, DC // 8)
        SPW = DC // NSPL

        h_done = []
        sx_free = 0
        for t in range(NTCH):
            par = t % 2
            c0 = t * TCH
            xw_prev = [(s_sx, 2 * DC * (t - 1))] if t >= 2 else []
            hv_prev = [(s_h, h_done[t - 2])] if t >= 2 else []
            for q in range(4):
                Q = t * 4 + q
                pA.op("act" if Q % 2 == 0 else "sp",
                      (lambda c0, q, Q: lambda e: e.dma_start(
                          xf32[Q % 2][:, :].rearrange("p (a c) -> p a c", a=QTR),
                          xT.ap()[q * QTR * 128:(q + 1) * QTR * 128, c0:c0 + TCH]
                          .rearrange("(a p) c -> p a c", p=128)))(c0, q, Q),
                      waits=[(s_xcast, max(0, Q - 1))], incs=[(s_xf[Q % 2], 16)])
                cwaits = [(s_xf[Q % 2], 16 * (Q // 2 + 1))]
                if q == 0:
                    cwaits += xw_prev + hv_prev
                pA.op("act", (lambda par, q, Q: lambda e: e.activation(
                    xbf[par][:, q * QTR * TCH:(q + 1) * QTR * TCH], xf32[Q % 2][:, :],
                    AF.Copy, bias=0.0, scale=1.0))(par, q, Q),
                    waits=cwaits, incs=[(s_xcast, 1)])
            for dc in range(DC):
                gi = t * DC + dc
                sq = dc % 2
                xcw = 4 * t + dc // QTR + 1
                pA.op("dve", (lambda par, dc, sq, xcw: lambda e: e.tensor_mul(
                    xsq[sq][:, :], xbf[par][:, dc * TCH:(dc + 1) * TCH],
                    xbf[par][:, dc * TCH:(dc + 1) * TCH]))(par, dc, sq, xcw),
                    waits=[(s_xcast, xcw)] + ([(s_sx, 2 * (gi - 1))] if gi >= 2 else []),
                    incs=[(s_xq, 1)])
                pA.op("pe", (lambda par, dc, xcw: lambda e: e.matmul(
                    sx_ps[:, 0:TCH], ones_col_bf[:, :], xbf[par][:, dc * TCH:(dc + 1) * TCH],
                    start=(dc == 0), stop=(dc == DC - 1)))(par, dc, xcw),
                    waits=[(s_xcast, xcw)] + ([(s_row, sx_free)] if dc == 0 else []),
                    incs=[(s_sx, 1)])
                pA.op("pe", (lambda dc, sq: lambda e: e.matmul(
                    sx2_ps[:, 0:TCH], ones_col_bf[:, :], xsq[sq][:, :],
                    start=(dc == 0), stop=(dc == DC - 1)))(dc, sq),
                    waits=[(s_xq, gi + 1)],
                    incs=[(s_sx, 1)])
            smm = s_sx.n
            m_r, v_r, r_r, b_r, t_r = [slice(i * TCH, (i + 1) * TCH) for i in range(5)]
            pA.op("dve", lambda e: e.tensor_scalar(rowb[0:1, m_r], sx_ps[:, 0:TCH], 1.0 / D, None, ALU.mult),
                  waits=[(s_sx, smm)], incs=[(s_row, 1)])
            pA.op("dve", lambda e: e.tensor_scalar(rowb[0:1, v_r], sx2_ps[:, 0:TCH], 1.0 / D, None, ALU.mult),
                  incs=[(s_row, 1)])
            sx_free = s_row.n
            pA.op("dve", lambda e: e.tensor_mul(rowb[0:1, t_r], rowb[0:1, m_r], rowb[0:1, m_r]),
                  waits=[(s_row, s_row.n)], incs=[(s_row, 1)])
            pA.op("dve", lambda e: e.tensor_sub(rowb[0:1, v_r], rowb[0:1, v_r], rowb[0:1, t_r]),
                  waits=[(s_row, s_row.n)], incs=[(s_row, 1)])
            rw0 = s_row.n
            pA.op("act", lambda e: e.activation(rowb[0:1, t_r], rowb[0:1, v_r], AF.Ln,
                                                bias=eps_t[0:1, 0:1], scale=1.0),
                  waits=[(s_row, rw0)], incs=[(s_row, 1)])
            pA.op("act", lambda e: e.activation(rowb[0:1, r_r], rowb[0:1, t_r], AF.Exp,
                                                bias=0.0, scale=-0.5),
                  waits=[(s_row, s_row.n)], incs=[(s_row, 1)])
            rw1 = s_row.n
            pA.op("dve", lambda e: e.tensor_mul(rowb[0:1, b_r], rowb[0:1, m_r], rowb[0:1, r_r]),
                  waits=[(s_row, rw1)], incs=[(s_row, 1)])
            pA.op("dve", lambda e: e.tensor_scalar(rowb[0:1, b_r], rowb[0:1, b_r], -1.0, None, ALU.mult),
                  waits=[(s_row, s_row.n)], incs=[(s_row, 1)])
            rw2 = s_row.n
            pA.op("pe", lambda e: e.matmul(bc_ps[:, 0:TCH], ones_row_f[0:1, :], rowb[0:1, r_r],
                                           start=True, stop=True),
                  waits=[(s_row, rw2), (s_bce, 2 * t)], incs=[(s_bcm, 1)])
            pA.op("dve", (lambda par: lambda e: e.tensor_copy(a_bc[par][:, :], bc_ps[:, 0:TCH]))(par),
                  waits=[(s_bcm, 2 * t + 1)], incs=[(s_bce, 1)])
            pA.op("pe", lambda e: e.matmul(bc_ps[:, 0:TCH], ones_row_f[0:1, :], rowb[0:1, b_r],
                                           start=True, stop=True),
                  waits=[(s_bce, 2 * t + 1)], incs=[(s_bcm, 1)])
            pA.op("dve", (lambda par: lambda e: e.tensor_copy(b_bc[par][:, :], bc_ps[:, 0:TCH]))(par),
                  waits=[(s_bcm, 2 * t + 2)], incs=[(s_bce, 1)])
            bce = s_bce.n
            hw_guard = [(s_hw, 16 * NSPL * t)] if t >= 2 else []
            for dc in range(DC):
                pA.op("dve", (lambda par, dc: lambda e: e.tensor_mul(
                    hbf[par][:, dc * TCH:(dc + 1) * TCH], xbf[par][:, dc * TCH:(dc + 1) * TCH],
                    a_bc[par][:, :]))(par, dc),
                    waits=([(s_bce, bce)] + hw_guard) if dc == 0 else [], incs=[(s_h, 1)])
                pA.op("dve", (lambda par, dc: lambda e: e.tensor_add(
                    hbf[par][:, dc * TCH:(dc + 1) * TCH], hbf[par][:, dc * TCH:(dc + 1) * TCH],
                    b_bc[par][:, :]))(par, dc),
                    waits=[(s_h, s_h.n)], incs=[(s_h, 1)])
            pA.op("dve", lambda e: e.drain(), incs=[(s_h, 1)])
            h_done.append(s_h.n)
            for spl in range(NSPL):
                d0, d1 = spl * SPW, (spl + 1) * SPW
                pA.op("sp", (lambda par, c0, d0, d1: lambda e: e.dma_start(
                    hT_scr.ap()[d0 * 128:d1 * 128, c0:c0 + TCH].rearrange("(a p) c -> p a c", p=128),
                    hbf[par][:, d0 * TCH:d1 * TCH].rearrange("p (a c) -> p a c", a=d1 - d0)))(par, c0, d0, d1),
                    waits=[(s_h, h_done[-1])] if spl == 0 else [], incs=[(s_hw, 16)])
        pA.op("sp", lambda e: e.nop(),
              waits=[(s_hw, 16 * NSPL * NTCH), (s_xf[0], s_xf[0].n), (s_xf[1], s_xf[1].n)])

        with nc.Block() as blkA:
            pA.emit(blkA)

    if "B" not in PHASES:
        _finish_debug(nc, out_h, D, TOWN)
        es.close()
        return nc

    UPG = HG * NQC
    with ExitStack() as esB:
        sbB = lambda n, s, d: esB.enter_context(nc.sbuf_tensor(n, s, d))
        psB = lambda n, s, d: esB.enter_context(nc.psum_tensor(n, _pad_psum(s, d), d))
        hp = [sbB(f"hp{i}", [128, DC * TP], BF16) for i in range(2)]
        wq_b = sbB("wq_b", [128, DC * HG * 128], BF16)
        wk_b = sbB("wk_b", [128, DC * HG * 128], BF16)
        wv_b = sbB("wv_b", [128, DC * HG * 128], BF16)
        KT = [[sbB(f"KT{p}{h}", [128, SB], BF16) for h in range(HG)] for p in range(2)]
        QT = [[sbB(f"QT{p}{h}", [128, TOWN], BF16) for h in range(HG)] for p in range(2)]
        VA = [[sbB(f"VA{p}{h}", [128, NKT * 128], BF16) for h in range(HG)] for p in range(2)]
        AT = [sbB(f"AT{i}", [128, KTG * QC], BF16) for i in range(2)]
        acc = sbB("acc", [128, KTG * QC], F32)
        vstg = [sbB(f"vstg{i}", [128, TP], BF16) for i in range(2)]
        ostg = [sbB(f"ostg{i}", [128, QC], BF16) for i in range(2)]
        negm = sbB("negm", [1, 4], F32)
        dmx = sbB("dmx", [1, 1], F32)
        lnm = sbB("lnm", [1, 1], F32)
        biasr = [sbB(f"biasr{i}", [128, 1], F32) for i in range(4)]
        rdn_bc = sbB("rdn_bc", [128, QC], F32)

        st_ps = [psB(f"st_ps{i}", [128, KTG * QC], F32) for i in range(2)]
        qk_ps = [psB(f"qk_ps{i}", [128, TP], F32) for i in range(2)]
        ov_ps = [psB(f"ov_ps{i}", [128, QC], F32) for i in range(2)]

        s_hp = [sem("s_hp0"), sem("s_hp1")]
        s_w = [[sem(f"s_w{p_}{q_}") for q_ in range(2)] for p_ in range(3)]
        s_qm = sem("s_qm"); s_qe = sem("s_qe"); s_stm = sem("s_stm")
        s_ex = sem("s_ex"); s_eu = sem("s_eu"); s_gm = sem("s_gm"); s_bw = sem("s_bw")
        s_br = sem("s_br"); s_av = sem("s_av"); s_avk = sem("s_avk"); s_oe = sem("s_oe")
        s_ob = [sem(f"s_ob{i_}") for i_ in range(2)]
        s_vw = [sem("s_vw0"), sem("s_vw1")]; s_vt = sem("s_vt")
        s_dacc = sem("s_dacc"); s_dn = sem("s_dn"); s_tp = sem("s_tp")
        s_w32 = sem("s_w32")

        pB = Prog()
        NSPL = max(1, DC // 8)
        SPW = DC // NSPL

        st_user = [(0, 0), (0, 0)]
        at_guard = [0, 0]
        qk_user = [0, 0]
        state = {"unit": 0, "bank": 1}
        qe_step_done = {}
        vt_done = {}
        hp_free = {}
        w_free = {}
        vw_cnt = [0, 0]

        def load_w(g, which):
            wt, buf = [(WqT, wq_b), (WkT, wk_b), (WvT, wv_b)][which]
            wfv = w_free.get((which, g - 1), 0)
            for spl in range(NSPL):
                d0, d1 = spl * SPW, (spl + 1) * SPW
                pB.op("pool", (lambda wt, buf, g, d0, d1: lambda e: e.dma_start(
                    buf[:, d0 * HG * 128:d1 * HG * 128].rearrange("p (a c) -> p a c", a=d1 - d0),
                    wt.ap()[d0 * 128:d1 * 128, g * HG * 128:(g + 1) * HG * 128]
                    .rearrange("(a p) c -> p a c", p=128)))(wt, buf, g, d0, d1),
                    waits=[(s_qm, wfv)] if spl == 0 else [],
                    incs=[(s_w[which][g % 2], 16)])

        def load_piece(g, tp):
            idx = g * NTP + tp
            for spl in range(NSPL):
                d0, d1 = spl * SPW, (spl + 1) * SPW
                pB.op("sp", (lambda idx, tp, d0, d1: lambda e: e.dma_start(
                    hp[idx % 2][:, d0 * TP:d1 * TP].rearrange("p (a c) -> p a c", a=d1 - d0),
                    hT_scr.ap()[d0 * 128:d1 * 128, tp * TP:(tp + 1) * TP]
                    .rearrange("(a p) c -> p a c", p=128)))(idx, tp, d0, d1),
                    waits=[(s_qm, hp_free.get(idx - 2, 0))] if spl == 0 else [],
                    incs=[(s_hp[idx % 2], 16)])

        def qkv_piece(g, tp):
            idx = g * NTP + tp
            par = g % 2
            wv_ = 16 * NSPL * (g // 2 + 1)
            hp_need = 16 * NSPL * (idx // 2 + 1)
            groups = [("k", h) for h in range(HG)] + [("v", h) for h in range(HG)]
            if tp < NOWNP:
                groups += [("q", h) for h in range(HG)]
            for gi, (kind, hj) in enumerate(groups):
                bank = (state["bank"] + 1) % 2
                state["bank"] = bank
                wsel = {"k": wk_b, "v": wv_b, "q": wq_b}[kind]
                for dc in range(DC):
                    fn = (lambda idx, hj, dc, bank, wsel: lambda e: e.matmul(
                        qk_ps[bank][:, 0:TP],
                        wsel[:, (dc * HG + hj) * 128:(dc * HG + hj + 1) * 128],
                        hp[idx % 2][:, dc * TP:(dc + 1) * TP],
                        start=(dc == 0), stop=(dc == DC - 1)))(idx, hj, dc, bank, wsel)
                    waits = []
                    if dc == 0:
                        waits = [(s_qe, qk_user[bank]), (s_hp[idx % 2], hp_need),
                                 (s_w[0][g % 2], wv_), (s_w[1][g % 2], wv_), (s_w[2][g % 2], wv_)]
                    incs = [(s_qm, 1)] if dc == DC - 1 else []
                    pB.op("pe", fn, waits=waits, incs=incs)
                if gi == len(groups) - 1:
                    hp_free[idx] = s_qm.n
                if kind == "q" and hj == HG - 1 and tp == NOWNP - 1:
                    w_free[(0, g)] = s_qm.n
                if kind == "k" and hj == HG - 1 and tp == NTP - 1:
                    w_free[(1, g)] = s_qm.n
                if kind == "v" and hj == HG - 1 and tp == NTP - 1:
                    w_free[(2, g)] = s_qm.n
                qmv = s_qm.n
                if kind == "k":
                    pB.op("dve", (lambda par, hj, tp, bank: lambda e: e.tensor_copy(
                        KT[par][hj][:, tp * TP:(tp + 1) * TP], qk_ps[bank][:, 0:TP]))(par, hj, tp, bank),
                        waits=[(s_qm, qmv)], incs=[(s_qe, 1)])
                elif kind == "q":
                    pB.op("dve", (lambda par, hj, tp, bank: lambda e: e.tensor_scalar(
                        QT[par][hj][:, tp * TP:(tp + 1) * TP], qk_ps[bank][:, 0:TP],
                        SCALE, None, ALU.mult))(par, hj, tp, bank),
                        waits=[(s_qm, qmv)], incs=[(s_qe, 1)])
                else:
                    vp = (idx * HG + hj) % 2
                    pB.op("dve", (lambda vp, bank: lambda e: e.tensor_copy(
                        vstg[vp][:, :], qk_ps[bank][:, 0:TP]))(vp, bank),
                        waits=[(s_qm, qmv), (s_vw[vp], vw_cnt[vp])],
                        incs=[(s_qe, 1)])
                    qev = s_qe.n
                    pB.op("sp", (lambda g, hj, tp, vp: lambda e: e.dma_start(
                        VT_scr.ap()[(g * HG + hj) * 128:(g * HG + hj + 1) * 128,
                                    tp * TP:(tp + 1) * TP],
                        vstg[vp][:, :]))(g, hj, tp, vp),
                        waits=[(s_qe, qev)], incs=[(s_vw[vp], 16)])
                    vw_cnt[vp] += 16
                qk_user[bank] = s_qe.n

        def emit_vtrans(g):
            par = g % 2
            first = True
            for hj in range(HG):
                for kt in range(NKT):
                    waits = []
                    if first:
                        waits = [(s_vw[0], vw_cnt[0]), (s_vw[1], vw_cnt[1]),
                                 (s_av, max(0, (g - 1) * UPG))]
                        first = False
                    pB.op("sp", (lambda g, hj, kt, par: lambda e: e.dma_start(
                        VA[par][hj][:, kt * 128:(kt + 1) * 128],
                        VT_scr.ap()[(g * HG + hj) * 128:(g * HG + hj + 1) * 128,
                                    kt * 128:(kt + 1) * 128],
                        transpose=True))(g, hj, kt, par),
                        waits=waits, incs=[(s_vt, 16)])
            vt_done[g] = s_vt.n

        def attn_unit(g, hh, qc):
            u = state["unit"]
            state["unit"] = u + 1
            par = g % 2
            bias_ap = zeros_col if (u < 2 or ZB) else biasr[u % 4]
            exvs = {}

            def emit_av(ktg):
                for j in range(KTG):
                    kt = ktg * KTG + j
                    waits = []
                    if j == 0:
                        waits = [(s_ex, exvs[ktg])]
                        if ktg == 0:
                            waits += [(s_oe, max(0, u - 1))]
                    incs = []
                    if kt == NKT - 1:
                        incs = [(s_av, 1)]
                    elif j == KTG - 1 and ktg < NKTG - 1:
                        incs = [(s_avk, 1)]
                    pB.op("pe", (lambda par, hh, qc, kt, j, ktg, u: lambda e: e.matmul(
                        ov_ps[u % 2][:, 0:QC],
                        VA[par][hh][:, kt * 128:(kt + 1) * 128],
                        AT[ktg % 2][:, j * QC:(j + 1) * QC],
                        start=(kt == 0), stop=(kt == NKT - 1)))(par, hh, qc, kt, j, ktg, u),
                        waits=waits, incs=incs)

            for ktg in range(NKTG):
                buf = ktg % 2
                for j in range(KTG):
                    kt = ktg * KTG + j
                    waits = []
                    if j == 0:
                        waits = [(s_ex, st_user[buf][0]), (s_dn, st_user[buf][1])]
                        if ktg == 0:
                            waits += [(s_qe, qe_step_done[g]), (s_vt, vt_done[g])]
                    pB.op("pe", (lambda par, hh, qc, kt, j, buf: lambda e: e.matmul(
                        st_ps[buf][:, j * QC:(j + 1) * QC],
                        KT[par][hh][:, kt * 128:(kt + 1) * 128],
                        QT[par][hh][:, qc * QC:(qc + 1) * QC],
                        start=True, stop=True))(par, hh, qc, kt, j, buf),
                        waits=waits, incs=[(s_stm, 1)] if j == KTG - 1 else [])
                stv = s_stm.n
                ewaits = [(s_stm, stv), (s_dacc, at_guard[ktg % 2])]
                if ktg == 0:
                    ewaits += [(s_av, u)]
                    if u >= 2 and not ZB:
                        ewaits += [(s_br, s_br.n)]
                elif ktg >= 2:
                    ewaits += [(s_avk, (NKTG - 1) * u + ktg - 1)]
                pB.op("act", (lambda buf, ktg, bias_ap: lambda e: e.activation(
                    AT[ktg % 2][:, :], st_ps[buf][:, 0:KTG * QC], AF.Exp,
                    bias=bias_ap[:, 0:1], scale=1.0))(buf, ktg, bias_ap),
                    waits=ewaits, incs=[(s_ex, 1)])
                if ktg == NKTG - 1:
                    pB.op("act", lambda e: e.nop(), incs=[(s_eu, 1)])
                st_user[buf] = (s_ex.n, st_user[buf][1])
                exv = s_ex.n
                if ktg == 0:
                    pB.op("dve", lambda e: e.tensor_copy(acc[:, :], AT[0][:, :]),
                          waits=[(s_ex, exv), (s_dacc, s_dacc.n)], incs=[(s_dacc, 1)])
                else:
                    pB.op("dve", (lambda ktg: lambda e: e.tensor_add(
                        acc[:, :], acc[:, :], AT[ktg % 2][:, :]))(ktg),
                        waits=[(s_ex, exv), (s_dacc, s_dacc.n)], incs=[(s_dacc, 1)])
                at_guard[ktg % 2] = s_dacc.n
                exvs[ktg] = exv
                if ktg >= 1:
                    emit_av(ktg - 1)
            emit_av(NKTG - 1)
            fv = s_dacc.n
            pB.op("pe", lambda e: e.matmul(
                st_ps[1][0:1, 0:QC], ones_col_f[:, :], acc[:, 0:QC],
                start=True, stop=False),
                waits=[(s_dacc, fv), (s_ex, st_user[1][0]), (s_dn, st_user[1][1])])
            pB.op("pe", lambda e: e.matmul(
                st_ps[1][0:1, 0:QC], ones_col_f[:, :], acc[:, QC:2 * QC],
                start=False, stop=True),
                incs=[(s_tp, 1)])
            denv = s_tp.n
            pB.op("dve", lambda e: e.reciprocal(acc[0:1, 0:QC], st_ps[1][0:1, 0:QC]),
                  waits=[(s_tp, denv)], incs=[(s_dn, 1)])
            rv_ = s_dn.n
            if not ZB:
                pB.op("dve", lambda e: e.reduce_max(
                    dmx[0:1, 0:1], st_ps[1][0:1, 0:QC], AX.X),
                    waits=[(s_tp, denv)], incs=[(s_dn, 1)])
                rmv = s_dn.n
                pB.op("act", lambda e: e.activation(
                    lnm[0:1, 0:1], dmx[0:1, 0:1], AF.Ln, bias=0.0, scale=1.0),
                    waits=[(s_dn, rmv)], incs=[(s_gm, 1)])
                gv1 = s_gm.n
                prev = zeros_col[0:1, 0:1] if u < 2 else \
                    negm[0:1, (u - 2) % 4:(u - 2) % 4 + 1]
                pB.op("dve", (lambda un, prev: lambda e: e.tensor_tensor(
                    negm[0:1, un:un + 1], prev, lnm[0:1, 0:1], ALU.subtract))(u % 4, prev),
                    waits=[(s_gm, gv1), (s_bw, 16 * max(0, u - 3))], incs=[(s_gm, 1)])
                gv2 = s_gm.n
                pB.op("act", (lambda u, un: lambda e: e.dma_start(
                    bias_scr.ap()[u:u + 1, 0:1], negm[0:1, un:un + 1]))(u, u % 4),
                    waits=[(s_gm, gv2)], incs=[(s_bw, 16)])
                bwv = s_bw.n
                pB.op("act", (lambda u: lambda e: e.dma_start(
                    biasr[(u + 2) % 4][:, 0:1], bass.AP(bias_scr, u * 128, [[0, 128], [1, 1]])))(u),
                    waits=[(s_bw, bwv), (s_eu, max(0, u - 1))], incs=[(s_br, 16)])
            pB.op("pe", lambda e: e.matmul(
                st_ps[1][:, QC:2 * QC], ones_row_f[0:1, :], acc[0:1, 0:QC],
                start=True, stop=True),
                waits=[(s_dn, rv_)], incs=[(s_tp, 1)])
            pB.op("dve", lambda e: e.tensor_copy(rdn_bc[:, :], st_ps[1][:, QC:2 * QC]),
                  waits=[(s_tp, s_tp.n)], incs=[(s_dn, 1)])
            st_user[1] = (st_user[1][0], s_dn.n)
            bcv = s_dn.n
            pB.op("dve", (lambda u: lambda e: e.tensor_mul(
                ostg[u % 2][:, :], ov_ps[u % 2][:, 0:QC], rdn_bc[:, :]))(u),
                waits=[(s_dn, bcv), (s_av, u + 1), (s_ob[u % 2], 16 * (u // 2))],
                incs=[(s_oe, 1)])
            oev = s_oe.n
            pB.op("sp", (lambda g, hh, qc, u: lambda e: e.dma_start(
                O_scr.ap()[(g * HG + hh) * 128:(g * HG + hh + 1) * 128,
                           qc * QC:(qc + 1) * QC],
                ostg[u % 2][:, :]))(g, hh, qc, u),
                waits=[(s_oe, oev)], incs=[(s_ob[u % 2], 16)])

        NSTR = 16
        RS = D // NSTR
        pcast = [(WoT, Wo_scr, r_) for r_ in range(NSTR)] + \
                [(WfT, Wf_scr, r_) for r_ in range(NSTR)]
        pci = {"i": 0}

        def emit_precast32(n):
            for _ in range(n):
                i = pci["i"]
                if i >= len(pcast):
                    return
                wsrc, wdst, r_ = pcast[i]
                pB.op("pool", (lambda wsrc, wdst, r_: lambda e: e.dma_start(
                    wdst.ap()[r_ * RS:(r_ + 1) * RS, :],
                    wsrc.ap()[r_ * RS:(r_ + 1) * RS, :]))(wsrc, wdst, r_),
                    incs=[(s_w32, 16)])
                pci["i"] = i + 1

        load_w(0, 0); load_w(0, 1); load_w(0, 2)
        load_piece(0, 0)
        if NTP > 1:
            load_piece(0, 1)
        for g in range(NG + 1):
            for tp in range(NTP):
                if g < NG:
                    qkv_piece(g, tp)
                    nxt = g * NTP + tp + 2
                    if nxt < NG * NTP:
                        load_piece(nxt // NTP, nxt % NTP)
                    if g + 1 < NG and tp == min(NOWNP, NTP - 1):
                        load_w(g + 1, 0)
                if g >= 1 and not NOATTN:
                    for k in range(max(1, UPG // NTP)):
                        uu = tp * max(1, UPG // NTP) + k
                        if uu < UPG:
                            attn_unit(g - 1, uu % HG, uu // HG)
            if g < NG:
                qe_step_done[g] = s_qe.n
                emit_vtrans(g)
                emit_precast32(4)
                if g + 1 < NG:
                    load_w(g + 1, 1)
                    load_w(g + 1, 2)
        pB.op("dve", lambda e: e.drain(), incs=[(s_oe, 1)])
        emit_precast32(len(pcast))
        endw = [(s_ob[i_], s_ob[i_].n) for i_ in range(2)]
        endw += [(s_w32, s_w32.n)]
        endw += [(s_br, s_br.n), (s_bw, s_bw.n),
                (s_vw[0], s_vw[0].n), (s_vw[1], s_vw[1].n), (s_vt, s_vt.n)]
        endw += [(s_hp[i_], s_hp[i_].n) for i_ in range(2)]
        endw += [(s_w[p_][q_], s_w[p_][q_].n) for p_ in range(3) for q_ in range(2)]
        pB.op("sp", lambda e: e.nop(), waits=endw)

        with nc.Block() as blkB:
            pB.emit(blkB)
    if "D" not in PHASES:
        _finish_debug(nc, out_h, D, TOWN)
        es.close()
        return nc

    with ExitStack() as esD:
        sbD = lambda n, s, d: esD.enter_context(nc.sbuf_tensor(n, s, d))
        psD = lambda n, s, d: esD.enter_context(nc.psum_tensor(n, _pad_psum(s, d), d))
        x1T = sbD("x1T", [128, DC * TC], F32)
        h2T = sbD("h2T", [128, DC * TC], BF16)
        OT = sbD("OT", [128, DC * TC], BF16)
        NWOB = DC if cfg.get("DBG4") else 2
        wo_b = [sbD(f"wo_b{i}", [128, DC * 128], BF16) for i in range(NWOB)]
        wf_b = wo_b
        as_b = sbD("as_b", [128, DC * ER], BF16)
        bf_b = sbD("bf_b", [128, ERC * D], BF16)
        xsl = [sbD(f"xsl{i}", [128, TC], F32) for i in range(2)]
        x1b = [sbD("x1b0", [128, TC], BF16)]
        xq2 = [sbD("xq20", [128, TC], BF16)]
        t1s = [sbD(f"t1s{i}", [128, TC], BF16) for i in range(ERC)]
        NOUT = 3
        outb = [sbD(f"outb{i}", [128, TC], F32) for i in range(NOUT)]
        psb = [sbD(f"psb{i}", [128, TC], F32) for i in range(2)]
        m_bc = sbD("m_bc", [128, TC], F32)
        r_bc = sbD("r_bc", [128, TC], F32)
        rows = sbD("rows", [1, 4 * TC], F32)
        biat = sbD("biat", [128, DC], F32)
        wlo = sbD("wlo", [16, 1], F32)
        whi = sbD("whi", [8, 1], F32)
        slo = sbD("slo", [16, 1], F32)
        shi = sbD("shi", [8, 1], F32)
        E16 = sbD("E16", [16, 128], F32)
        wex = [sbD(f"wex{i}", [128, 1], F32) for i in range(ERC)]

        p_ps = [psD(f"p_ps{i}", [128, TC], F32) for i in range(2)]
        f_ps = [psD(f"f_ps{i}", [128, TC], F32) for i in range(2)]
        sxD = psD("sxD", [1, TC], F32)
        sx2D = psD("sx2D", [1, TC], F32)
        u_ps = psD("u_ps", [128, TC], F32)
        u2_ps = psD("u2_ps", [128, TC], F32)

        s_ot = sem("s_ot"); s_wo = [sem("s_wo0"), sem("s_wo1")]
        s_xs = [sem("s_xs0"), sem("s_xs1")]
        s_pm = sem("s_pm"); s_pe2 = sem("s_pe2"); s_db = sem("s_db"); s_sx2 = sem("s_sx2")
        s_rw = sem("s_rw"); s_bc2 = sem("s_bc2"); s_be2 = sem("s_be2"); s_h2 = sem("s_h2")
        s_t1 = sem("s_t1"); s_t1e = sem("s_t1e"); s_fm = sem("s_fm"); s_fe = sem("s_fe")
        s_out = [sem(f"s_out{i_}") for i_ in range(4)]
        s_pc = sem("s_pc"); s_fc = sem("s_fc")
        s_wx = sem("s_wx"); s_io = sem("s_io"); s_wem = sem("s_wem"); s_wee = sem("s_wee")

        pD = Prog()
        NSPL = max(1, DC // 8)
        SPW = DC // NSPL

        for spl in range(NSPL):
            d0, d1 = spl * SPW, (spl + 1) * SPW
            pD.op("pool", (lambda d0, d1: lambda e: e.dma_start(
                as_b[:, d0 * ER:d1 * ER].rearrange("p (a c) -> p a c", a=d1 - d0),
                AsT.ap()[d0 * 128:d1 * 128, :].rearrange("(a p) c -> p a c", p=128)))(d0, d1),
                incs=[(s_wx, 16)])
        pD.op("pool", lambda e: e.dma_start(
            bf_b[:, :].rearrange("p (a c) -> p a c", a=ERC),
            BfL.ap().rearrange("(a p) c -> p a c", p=128)), incs=[(s_wx, 16)])
        pD.op("pool", lambda e: e.dma_start(biat[:, :], bias_h.ap()), incs=[(s_wx, 16)])
        pD.op("pool", lambda e: e.dma_start(wlo[:, :], cs_h.ap()[0:16, 0:1], allow_slow_non_contiguous=True),
              waits=[(s_wx, s_wx.n)], incs=[(s_wx, 16)])
        pD.op("pool", lambda e: e.dma_start(whi[:, :], cs_h.ap()[8:16, 0:1], allow_slow_non_contiguous=True),
              waits=[(s_wx, s_wx.n)], incs=[(s_wx, 16)])
        pD.op("pool", lambda e: e.dma_start(slo[:, :], cs_h.ap()[0:16, 1:2], allow_slow_non_contiguous=True),
              waits=[(s_wx, s_wx.n)], incs=[(s_wx, 16)])
        pD.op("pool", lambda e: e.dma_start(shi[:, :], cs_h.ap()[8:16, 1:2], allow_slow_non_contiguous=True),
              waits=[(s_wx, s_wx.n)], incs=[(s_wx, 16)])
        wxv = s_wx.n
        Ea = psb[0][0:16, 0:128]
        Eb = psb[1][0:16, 0:128]
        pD.op("pool", lambda e: e.iota(Ea, [[1, 128]], base=0, channel_multiplier=-16,
                                       allow_small_or_imprecise_dtypes=True), incs=[(s_io, 1)])
        pD.op("dve", lambda e: e.tensor_scalar(E16[:, :], Ea, 0.0, None, ALU.is_ge),
              waits=[(s_wx, wxv), (s_io, 1)], incs=[(s_rw, 1)])
        pD.op("dve", lambda e: e.tensor_scalar(Eb, Ea, 15.5, None, ALU.is_le),
              waits=[(s_rw, s_rw.n)], incs=[(s_rw, 1)])
        pD.op("dve", lambda e: e.tensor_mul(E16[:, :], E16[:, :], Eb),
              waits=[(s_rw, s_rw.n)], incs=[(s_rw, 1)])
        pD.op("dve", lambda e: e.tensor_mul(wlo[:, :], wlo[:, :], slo[:, :]),
              waits=[(s_wx, wxv)], incs=[(s_rw, 1)])
        pD.op("dve", lambda e: e.tensor_mul(whi[:, :], whi[:, :], shi[:, :]), incs=[(s_rw, 1)])
        rv = s_rw.n
        pD.op("pe", lambda e: e.matmul(u_ps[:, 0:1], E16[:, :], wlo[:, 0:1], start=True, stop=True),
              waits=[(s_rw, rv)], incs=[(s_wem, 1)])
        pD.op("pe", lambda e: e.matmul(u2_ps[:, 0:1], E16[0:8, :], whi[:, 0:1], start=True, stop=True),
              incs=[(s_wem, 1)])
        pD.op("dve", lambda e: e.tensor_copy(wex[0][:, :], u_ps[:, 0:1]),
              waits=[(s_wem, 2)], incs=[(s_wee, 1)])
        pD.op("dve", lambda e: e.tensor_copy(wex[1][:, :], u2_ps[:, 0:1]), incs=[(s_wee, 1)])

        p_user = [0, 0]

        pe2v = {}
        f_user = [0, 0]
        pc_user = [0, 0]
        fc_user = [0, 0]

        sx_free2 = 0
        prev_fe = 0
        for tc in range(NTC):
            for dhc in range(DC):
                pD.op("sp", (lambda tc, dhc: lambda e: e.dma_start(
                    OT[:, dhc * TC:(dhc + 1) * TC],
                    O_scr.ap()[dhc * 128:(dhc + 1) * 128, tc * TC:(tc + 1) * TC]))(tc, dhc),
                    waits=[(s_pm, tc * DC)] if dhc == 0 else [], incs=[(s_ot, 16)])
            otv = s_ot.n
            if DBG2 and tc == 0:
                for dhc in range(DC):
                    pD.op("sp", (lambda dhc: lambda e: e.dma_start(
                        ot_dbg.ap()[dhc * 128:(dhc + 1) * 128, :],
                        OT[:, dhc * TC:(dhc + 1) * TC]))(dhc),
                        waits=[(s_ot, otv)] if dhc == 0 else [], incs=[(s_out[1], 16)])
            def emit_stats(tc, oc):
                gi = tc * DC + oc
                pD.op("dve", (lambda oc, gi: lambda e: e.tensor_copy(
                    x1b[0][:, :], x1T[:, oc * TC:(oc + 1) * TC]))(oc, gi),
                    waits=([(s_sx2, 2 * gi)] if gi >= 1 else []) + [(s_pe2, pe2v[gi])],
                    incs=[(s_db, 1)])
                pD.op("dve", (lambda oc: lambda e: e.tensor_mul(
                    xq2[0][:, :], x1b[0][:, :], x1b[0][:, :]))(oc),
                    waits=[(s_db, s_db.n)], incs=[(s_db, 1)])
                dbv = s_db.n
                pD.op("pe", (lambda oc: lambda e: e.matmul(
                    sxD[:, 0:TC], ones_col_bf[:, :], x1b[0][:, :],
                    start=(oc == 0), stop=(oc == DC - 1)))(oc),
                    waits=[(s_db, dbv)] + ([(s_rw, sx_free2)] if oc == 0 else []),
                    incs=[(s_sx2, 1)])
                pD.op("pe", (lambda oc: lambda e: e.matmul(
                    sx2D[:, 0:TC], ones_col_bf[:, :], xq2[0][:, :],
                    start=(oc == 0), stop=(oc == DC - 1)))(oc),
                    incs=[(s_sx2, 1)])

            for oc in range(DC):
                wo_ord = tc * DC + oc
                uni = tc * 2 * DC + oc
                pD.op("pool", (lambda oc, par=oc % 2: lambda e: e.dma_start(
                    wo_b[oc % NWOB][:, :].rearrange("p (a c) -> p a c", a=DC),
                    Wo_scr.ap()[:, oc * 128:(oc + 1) * 128].rearrange("(a p) c -> p a c", p=128)))(oc),
                    waits=[(s_pm, max(0, wo_ord - 1)), (s_fm, tc * DC)],
                    incs=[(s_wo[oc % 2], 16)])
                pD.op("pool", (lambda tc, oc: lambda e: e.dma_start(
                    xsl[oc % 2][:, :], xT.ap()[oc * 128:(oc + 1) * 128, tc * TC:(tc + 1) * TC]))(tc, oc),
                    waits=[(s_pe2, p_user[oc % 2])], incs=[(s_xs[oc % 2], 16)])
                bank = oc % 2
                for dhc in range(DC):
                    waits = []
                    if dhc == 0:
                        waits = [(s_ot, otv), (s_wo[oc % 2], 16 * (uni // 2 + 1)),
                                 (s_pc, pc_user[bank])]
                    pD.op("pe", (lambda oc, dhc, bank: lambda e: e.matmul(
                        p_ps[bank][:, 0:TC], wo_b[oc % NWOB][:, dhc * 128:(dhc + 1) * 128],
                        OT[:, dhc * TC:(dhc + 1) * TC],
                        start=(dhc == 0), stop=(dhc == DC - 1)))(oc, dhc, bank),
                        waits=waits,
                        incs=([(s_pm, 1)] if dhc == DC - 1 else []))
                pmv = s_pm.n
                if DBG2 and tc == 0 and oc == 2:
                    pD.op("act", (lambda bank: lambda e: e.copy(
                        outb[2][:, :], p_ps[bank][:, 0:TC]))(bank),
                        waits=[(s_pm, pmv)], incs=[(s_io, 1)])
                    pD.op("act", lambda e: e.dma_start(ps_dbg.ap(), outb[2][:, :]),
                        waits=[(s_io, s_io.n)], incs=[(s_out[1], 16)])
                pD.op("act", (lambda oc, bank: lambda e: e.copy(
                    psb[oc % 2][:, :], p_ps[bank][:, 0:TC]))(oc, bank),
                    waits=[(s_pm, pmv), (s_pe2, p_user[oc % 2])] +
                          ([(s_fe, prev_fe)] if oc < 2 else []),
                    incs=[(s_pc, 1)])
                pcv = s_pc.n
                if DBG2 and tc == 0 and oc == 2:
                    pD.op("sp", lambda e: e.dma_start(xs_dbg.ap(), xsl[0][:, :]),
                          waits=[(s_xs[0], 32)], incs=[(s_out[2], 16)])
                pD.op("dve", (lambda oc: lambda e: e.tensor_add(
                    x1T[:, oc * TC:(oc + 1) * TC], psb[oc % 2][:, :], xsl[oc % 2][:, :]))(oc),
                    waits=[(s_pc, pcv), (s_xs[oc % 2], 16 * ((tc * DC + oc) // 2 + 1))],
                    incs=[(s_pe2, 1)])
                p_user[bank] = s_pe2.n
                pc_user[bank] = pcv
                pe2v[tc * DC + oc] = s_pe2.n
                if oc >= 1:
                    emit_stats(tc, oc - 1)
            emit_stats(tc, DC - 1)
            sxv = s_sx2.n
            m_r, v_r, r_r, t_r = [slice(i * TC, (i + 1) * TC) for i in range(4)]
            b_r = t_r
            pD.op("dve", lambda e: e.tensor_scalar(rows[0:1, m_r], sxD[:, 0:TC], 1.0 / D, None, ALU.mult),
                  waits=[(s_sx2, sxv)], incs=[(s_rw, 1)])
            pD.op("dve", lambda e: e.tensor_scalar(rows[0:1, v_r], sx2D[:, 0:TC], 1.0 / D, None, ALU.mult),
                  incs=[(s_rw, 1)])
            sx_free2 = s_rw.n
            pD.op("dve", lambda e: e.tensor_mul(rows[0:1, t_r], rows[0:1, m_r], rows[0:1, m_r]),
                  waits=[(s_rw, s_rw.n)], incs=[(s_rw, 1)])
            pD.op("dve", lambda e: e.tensor_sub(rows[0:1, v_r], rows[0:1, v_r], rows[0:1, t_r]),
                  waits=[(s_rw, s_rw.n)], incs=[(s_rw, 1)])
            rw0 = s_rw.n
            pD.op("act", lambda e: e.activation(rows[0:1, t_r], rows[0:1, v_r], AF.Ln,
                                                bias=eps_t[0:1, 0:1], scale=1.0),
                  waits=[(s_rw, rw0)], incs=[(s_rw, 1)])
            pD.op("act", lambda e: e.activation(rows[0:1, r_r], rows[0:1, t_r], AF.Exp,
                                                bias=0.0, scale=-0.5),
                  waits=[(s_rw, s_rw.n)], incs=[(s_rw, 1)])
            rw1 = s_rw.n
            pD.op("dve", lambda e: e.tensor_mul(rows[0:1, b_r], rows[0:1, m_r], rows[0:1, r_r]),
                  waits=[(s_rw, rw1)], incs=[(s_rw, 1)])
            pD.op("dve", lambda e: e.tensor_scalar(rows[0:1, b_r], rows[0:1, b_r], -1.0, None, ALU.mult),
                  waits=[(s_rw, s_rw.n)], incs=[(s_rw, 1)])
            rw2 = s_rw.n
            pD.op("pe", lambda e: e.matmul(u_ps[:, 0:TC], ones_row_f[0:1, :], rows[0:1, r_r],
                                           start=True, stop=True),
                  waits=[(s_rw, rw2), (s_t1e, 2 * tc), (s_wee, 2)], incs=[(s_bc2, 1)])
            pD.op("pe", lambda e: e.matmul(u2_ps[:, 0:TC], ones_row_f[0:1, :], rows[0:1, b_r],
                                           start=True, stop=True),
                  incs=[(s_bc2, 1)])
            bcv = s_bc2.n
            pD.op("dve", lambda e: e.tensor_copy(r_bc[:, :], u_ps[:, 0:TC]),
                  waits=[(s_bc2, bcv)], incs=[(s_be2, 1)])
            pD.op("dve", lambda e: e.tensor_copy(m_bc[:, :], u2_ps[:, 0:TC]), incs=[(s_be2, 1)])
            bev = s_be2.n
            x1done = s_pe2.n
            for dc in range(DC):
                pD.op("dve", (lambda dc: lambda e: e.tensor_mul(
                    h2T[:, dc * TC:(dc + 1) * TC], x1T[:, dc * TC:(dc + 1) * TC], r_bc[:, :]))(dc),
                    waits=[(s_be2, bev), (s_pe2, x1done)] if dc == 0 else [], incs=[(s_h2, 1)])
                pD.op("dve", (lambda dc: lambda e: e.tensor_add(
                    h2T[:, dc * TC:(dc + 1) * TC], h2T[:, dc * TC:(dc + 1) * TC], m_bc[:, :]))(dc),
                    waits=[(s_h2, s_h2.n)], incs=[(s_h2, 1)])
            h2v = s_h2.n
            for erc in range(ERC):
                tps = u_ps if erc == 0 else u2_ps
                for dc in range(DC):
                    pD.op("pe", (lambda erc, dc, tps: lambda e: e.matmul(
                        tps[:, 0:TC], as_b[:, dc * ER + erc * 128:dc * ER + (erc + 1) * 128],
                        h2T[:, dc * TC:(dc + 1) * TC],
                        start=(dc == 0), stop=(dc == DC - 1)))(erc, dc, tps),
                        waits=[(s_h2, h2v), (s_be2, bev)] if dc == 0 else [],
                        incs=[(s_t1, 1)] if dc == DC - 1 else [])
                t1v = s_t1.n
                pD.op("dve", (lambda erc, tps: lambda e: e.tensor_scalar(
                    t1s[erc][:, :], tps[:, 0:TC], wex[erc][:, 0:1], None, ALU.mult))(erc, tps),
                    waits=[(s_t1, t1v), (s_wee, 2)], incs=[(s_t1e, 1)])
            t1ev = s_t1e.n
            if DBG2:
                for dc in range(DC):
                    pD.op("sp", (lambda tc, dc: lambda e: e.dma_start(
                        x1_dbg.ap()[dc * 128:(dc + 1) * 128, tc * TC:(tc + 1) * TC],
                        x1T[:, dc * TC:(dc + 1) * TC]))(tc, dc),
                        waits=[(s_h2, h2v)] if dc == 0 else [], incs=[(s_out[0], 16)])
                    pD.op("sp", (lambda tc, dc: lambda e: e.dma_start(
                        h2_dbg.ap()[dc * 128:(dc + 1) * 128, tc * TC:(tc + 1) * TC],
                        h2T[:, dc * TC:(dc + 1) * TC]))(tc, dc), incs=[(s_out[0], 16)])
                for erc in range(ERC):
                    pD.op("sp", (lambda tc, erc: lambda e: e.dma_start(
                        t1_dbg.ap()[erc * 128:(erc + 1) * 128, tc * TC:(tc + 1) * TC],
                        t1s[erc][:, :]))(tc, erc),
                        waits=[(s_t1e, t1ev)] if erc == 0 else [], incs=[(s_out[0], 16)])
            for oc in range(DC):
                wf_ord = tc * DC + oc
                uni = tc * 2 * DC + DC + oc
                pD.op("pool", (lambda oc, par=oc % 2: lambda e: e.dma_start(
                    wf_b[oc % NWOB][:, :].rearrange("p (a c) -> p a c", a=DC),
                    Wf_scr.ap()[:, oc * 128:(oc + 1) * 128].rearrange("(a p) c -> p a c", p=128)))(oc),
                    waits=[(s_fm, max(0, wf_ord - 1)), (s_pm, (tc + 1) * DC)],
                    incs=[(s_wo[oc % 2], 16)])
                bank = oc % 2
                for dc in range(DC):
                    waits = []
                    if dc == 0:
                        waits = [(s_wo[oc % 2], 16 * (uni // 2 + 1)), (s_fc, fc_user[bank]),
                                 (s_t1e, t1ev)]
                    pD.op("pe", (lambda oc, dc, bank: lambda e: e.matmul(
                        f_ps[bank][:, 0:TC], wf_b[oc % NWOB][:, dc * 128:(dc + 1) * 128],
                        h2T[:, dc * TC:(dc + 1) * TC],
                        start=(dc == 0), stop=False))(oc, dc, bank),
                        waits=waits)
                for erc in range(ERC):
                    pD.op("pe", (lambda oc, erc, bank: lambda e: e.matmul(
                        f_ps[bank][:, 0:TC], bf_b[:, erc * D + oc * 128:erc * D + (oc + 1) * 128],
                        t1s[erc][:, :],
                        start=False, stop=(erc == ERC - 1)))(oc, erc, bank),
                        incs=[(s_fm, 1)] if erc == ERC - 1 else [])
                fmv = s_fm.n
                og = tc * DC + oc
                ow = [(s_out[og % NOUT], 16 * (og // NOUT))] if og >= NOUT else []
                pD.op("act", (lambda oc, bank: lambda e: e.copy(
                    psb[oc % 2][:, :], f_ps[bank][:, 0:TC]))(oc, bank),
                    waits=[(s_fm, fmv), (s_fe, f_user[oc % 2])] +
                          ([(s_pe2, x1done)] if oc < 2 else []),
                    incs=[(s_fc, 1)])
                fcv = s_fc.n
                pD.op("dve", (lambda oc, og: lambda e: e.scalar_tensor_tensor(
                    outb[og % NOUT][:, :], psb[oc % 2][:, :], biat[:, oc:oc + 1],
                    x1T[:, oc * TC:(oc + 1) * TC], ALU.add, ALU.add))(oc, og),
                    waits=[(s_fc, fcv)] + ow, incs=[(s_fe, 1)])
                f_user[bank] = s_fe.n
                fc_user[bank] = fcv
                if oc >= 1:
                    pD.op("sp", (lambda tc, oc, og: lambda e: e.dma_start(
                        out_h.ap()[(oc - 1) * 128:oc * 128, tc * TC:(tc + 1) * TC],
                        outb[(og - 1) % NOUT][:, :]))(tc, oc, og),
                        waits=[(s_fe, s_fe.n)], incs=[(s_out[(og - 1) % NOUT], 16)])
            pD.op("dve", lambda e: e.drain(), incs=[(s_fe, 1)])
            ogl = tc * DC + DC - 1
            pD.op("sp", (lambda tc, ogl: lambda e: e.dma_start(
                out_h.ap()[(DC - 1) * 128:DC * 128, tc * TC:(tc + 1) * TC],
                outb[ogl % NOUT][:, :]))(tc, ogl),
                waits=[(s_fe, s_fe.n)], incs=[(s_out[ogl % NOUT], 16)])
            prev_fe = s_fe.n
        pD.op("sp", lambda e: e.nop(),
              waits=[(s_out[i_], s_out[i_].n) for i_ in range(4)])

        with nc.Block() as blkD:
            pD.emit(blkD)

    es.close()
    return nc


def _prep_inputs(x, Wq, Wk, Wv, Wo, Wf, bias, A, Bf, coeffs, scales, ncores=NCORES):
    x = np.asarray(x, dtype=np.float32)
    Bn, S, D = x.shape
    ER = A.shape[0] * A.shape[1]
    WqT = np.ascontiguousarray(np.asarray(Wq, np.float32).T)
    WkT = np.ascontiguousarray(np.asarray(Wk, np.float32).T)
    WvT = np.ascontiguousarray(np.asarray(Wv, np.float32).T)
    WoT = np.ascontiguousarray(np.asarray(Wo, np.float32).T)
    WfT = np.ascontiguousarray(np.asarray(Wf, np.float32).T)
    AsT = np.ascontiguousarray(np.asarray(A, np.float32).reshape(ER, D).T)
    BfLm = np.ascontiguousarray(np.asarray(Bf, np.float32).transpose(0, 2, 1).reshape(ER, D))
    biasm = np.ascontiguousarray(np.asarray(bias, np.float32).reshape(D // 128, 128).T)
    cs = np.zeros((32, 2), np.float32)
    cs[:16, 0] = np.asarray(coeffs, np.float32)
    cs[:16, 1] = np.asarray(scales, np.float32)
    TOWN = S // 2
    in_maps = []
    for c in range(ncores):
        b = (c // 2) % Bn
        half = c % 2
        xb = np.ascontiguousarray(x[b].T)
        if half == 0:
            xr = xb
        else:
            xr = np.ascontiguousarray(np.concatenate([xb[:, TOWN:], xb[:, :TOWN]], axis=1))
        in_maps.append(dict(xT=xr, WqT=WqT, WkT=WkT, WvT=WvT, WoT=WoT, WfT=WfT,
                            AsT=AsT, BfL=BfLm, bias=biasm, coefsc=cs))
    return in_maps, (Bn, S, D, TOWN)


def kernel(L0_attn_x, L0_attn_Wq, L0_attn_Wk, L0_attn_Wv, L0_attn_Wo,
           L0_ffn_W, L0_ffn_bias, L0_ffn_A, L0_ffn_Bf, L0_ffn_coeffs, L0_ffn_scales):
    in_maps, (Bn, S, D, TOWN) = _prep_inputs(
        L0_attn_x, L0_attn_Wq, L0_attn_Wk, L0_attn_Wv, L0_attn_Wo,
        L0_ffn_W, L0_ffn_bias, L0_ffn_A, L0_ffn_Bf, L0_ffn_coeffs, L0_ffn_scales)
    nc = build_program(default_cfg())
    res = run_bass_kernel_spmd(nc, in_maps, core_ids=list(range(NCORES)))
    out = np.empty((Bn, S, D), dtype=np.float32)
    for c in range(NCORES):
        b = c // 2
        half = c % 2
        o = res.results[c]["out"]
        out[b, half * TOWN:(half + 1) * TOWN, :] = np.asarray(o).T
    return out
